# version 04
# baseline (speedup 1.0000x reference)
import math
import numpy as np
import ml_dtypes
import concourse.bass as bass
import concourse.mybir as mybir
from concourse.bass_utils import run_bass_kernel_spmd

F32 = mybir.dt.float32
BF16 = mybir.dt.bfloat16
OP = mybir.AluOpType
AF = mybir.ActivationFunctionType
AX = mybir.AxisListType

S = 2048
D = 2048
NH = 64
HD = 64
DS = 128
CH = 64
NCH = S // CH
DIN = 4096
DPROJ = 8512
NEXP = 12
I_EXP = 1408
EPS = 1e-5

DEV_TAPS = ()
DEV_STOP = None

ENGINES = ("sync", "scalar", "vector", "tensor", "gpsimd")


class Sem:
    def __init__(self, h):
        self.h = h
        self.v = 0


class P:

    def __init__(self, nc, stack):
        self.nc = nc
        self.stack = stack
        self.streams = {e: [] for e in ENGINES}
        self.sems = {}
        self.barv = 0
        self._uid = 0
        self.HAZ_WIN = 2
        self.recent = {e: [] for e in ENGINES}
        self.junk = None

    @staticmethod
    def _apkey(ap):
        try:
            esz = {F32: 4, BF16: 2, mybir.dt.uint8: 1}.get(ap.dtype, 4)
            b0 = ap.offset * esz
            if ap.ndim > 2:
                return (ap.name, b0, b0 + (1 << 40))
            return (ap.name, b0, b0 + ap.free_nbytes())
        except Exception:
            return None

    @staticmethod
    def _overlap(k1, k2):
        return (k1 is not None and k2 is not None and k1[0] == k2[0]
                and k1[1] < k2[2] and k2[1] < k1[2])

    def _spacer(self, eng):
        j = self.junk
        if eng == "vector":
            self.emit(eng, lambda e: e.tensor_copy(j[:, 0:4], j[:, 4:8]))
        elif eng == "scalar":
            self.emit(eng, lambda e: e.copy(j[:, 0:4], j[:, 4:8]))
        elif eng == "gpsimd":
            self.emit(eng, lambda e: e.tensor_copy(j[:, 0:4], j[:, 4:8]))
        else:
            self.emit(eng, lambda e: e.nop())
        self.recent[eng].append(None)
        del self.recent[eng][:-self.HAZ_WIN]

    def sem(self, name):
        s = Sem(self.stack.enter_context(self.nc.semaphore(name)))
        self.sems[name] = s
        return s

    def emit(self, eng, fn):
        self.streams[eng].append(fn)

    def wait(self, eng, sem, val=None):
        v = sem.v if val is None else val
        self.emit(eng, lambda e, h=sem.h, v=v: e.wait_ge(h, v))

    def dma(self, q, out, in_, sem, inc=True, **kw):
        def fn(e, out=out, in_=in_, kw=kw, h=sem.h):
            e.dma_start(out, in_, **kw).then_inc(h, 16)
        self.emit(q, fn)
        sem.v += 16
        return sem.v

    def dmat(self, q, out, in_, sem):
        def fn(e, out=out, in_=in_, h=sem.h):
            e.dma_start(out, in_, transpose=True).then_inc(h, 16)
        self.emit(q, fn)
        sem.v += 16
        return sem.v

    def op(self, eng, method, *args, inc=None, **kw):
        aps = [a for a in args if isinstance(a, bass.AP)]
        kw_out = kw.get("out")
        if kw_out is not None:
            wr = kw_out
            rds = aps + [v for k, v in kw.items()
                         if k != "out" and isinstance(v, bass.AP)]
        else:
            wr = aps[0] if aps else None
            rds = aps[1:] + [v for v in kw.values() if isinstance(v, bass.AP)]
        rkeys = [self._apkey(a) for a in rds]
        guard = 0
        while any(self._overlap(rk, wk) for rk in rkeys
                  for wk in self.recent[eng]):
            self._spacer(eng)
            guard += 1
            assert guard <= self.HAZ_WIN + 1
        if inc is None:
            def fn(e, m=method, a=args, kw=kw):
                getattr(e, m)(*a, **kw)
        else:
            inc.v += 1

            def fn(e, m=method, a=args, kw=kw, h=inc.h):
                getattr(e, m)(*a, **kw).then_inc(h, 1)
        self.emit(eng, fn)
        self.recent[eng].append(self._apkey(wr))
        del self.recent[eng][:-self.HAZ_WIN]
        return inc.v if inc is not None else None

    def barrier(self):
        bar = self.sems["bar"]
        for eng in ENGINES:
            if eng in ("sync", "gpsimd", "scalar"):
                qs = self.sems["q_" + eng]
                self.wait(eng, qs, qs.v)
            self.emit(eng, lambda e, h=bar.h: e.nop().then_inc(h, 1))
        self.barv += 5
        for eng in ENGINES:
            self.wait(eng, bar, self.barv)

    def finalize(self, block):
        for eng in ENGINES:
            stream = self.streams[eng]
            getattr(block, eng)(lambda e, stream=stream: [f(e) for f in stream])


class Arena:

    def __init__(self, nc, stack, bytes_per_part=200 * 1024):
        self.cap = bytes_per_part
        self.t = stack.enter_context(
            nc.sbuf_tensor("arena", [128, bytes_per_part], mybir.dt.uint8))
        self.off = 0
        self.marks = []

    def alloc(self, shape, dt):
        esz = {F32: 4, BF16: 2, mybir.dt.int32: 4}[dt]
        free = int(np.prod(shape[1:]))
        nbytes = free * esz
        off = (self.off + 511) & ~511
        assert off + nbytes <= self.cap, f"SBUF arena overflow: {off + nbytes}"
        self.off = off + nbytes
        v = self.t[0:shape[0], off:off + nbytes].bitcast(dt)
        if len(shape) == 2:
            return v
        pat = "p (" + " ".join(f"d{i}" for i in range(1, len(shape))) + ") -> p " + \
              " ".join(f"d{i}" for i in range(1, len(shape)))
        return v.rearrange(pat, **{f"d{i}": shape[i] for i in range(1, len(shape) - 1)})

    def mark(self):
        self.marks.append(self.off)

    def release(self):
        self.off = self.marks.pop()


def ceil_div(a, b):
    return (a + b - 1) // b



def build_program(taps=(), stop=None):
    from contextlib import ExitStack
    stack = ExitStack()
    nc = bass.Bass("TRN2", target_bir_lowering=False)

    def dram(name, shape, dt, kind="Internal"):
        k = "ExternalOutput" if (name in taps and kind == "Internal") else kind
        return nc.dram_tensor(name, shape, dt, kind=k).ap()

    u_in = dram("u", [S, D], F32, "ExternalInput")
    in_projT = dram("in_projT", [D, DPROJ], F32, "ExternalInput")
    woutT = dram("woutT", [DIN, D], F32, "ExternalInput")
    gateT = dram("gateT", [D, 8], F32, "ExternalInput")
    sgT = dram("sgT", [128, 16], F32, "ExternalInput")
    wgT = dram("wgT", [8, D, I_EXP], F32, "ExternalInput")
    wuT = dram("wuT", [8, D, I_EXP], F32, "ExternalInput")
    wdT = dram("wdT", [8, I_EXP, D], F32, "ExternalInput")
    swgT = dram("swgT", [D, 4 * I_EXP], F32, "ExternalInput")
    swuT = dram("swuT", [D, 4 * I_EXP], F32, "ExternalInput")
    swdT = dram("swdT", [4 * I_EXP, D], F32, "ExternalInput")
    d_in = dram("Dh", [1, NH], F32, "ExternalInput")
    dtb_in = dram("dt_bias", [NH, 1], F32, "ExternalInput")
    alog_in = dram("A_log", [NH, 1], F32, "ExternalInput")
    bnw_in = dram("bnorm_w", [DS, 1], F32, "ExternalInput")
    cnw_in = dram("cnorm_w", [DS, 1], F32, "ExternalInput")
    lnw_in = dram("ln_w", [1, D], F32, "ExternalInput")
    lnb_in = dram("ln_b", [1, D], F32, "ExternalInput")

    out = dram("out", [S, D], F32, "ExternalOutput")

    xn_tm = dram("xn_tm", [S, D], F32)
    xn_fm32 = dram("xn_fm32", [D, S], F32)
    un_tm = dram("un_tm", [S, D], BF16)
    un_fm = dram("un_fm", [D, S], BF16)
    yg_tm = dram("yg_tm", [S, DIN], BF16)
    comb_fm = dram("comb_fm", [NEXP, S], F32)
    rt = dram("rt", [64, S], F32)

    p = P(nc, stack)
    for name in ("q_sync", "q_gpsimd", "q_scalar", "tp", "vp", "sp", "gp", "bar"):
        p.sem(name)
    qs, qg = p.sems["q_sync"], p.sems["q_gpsimd"]
    tp, vp, sp_, gp = p.sems["tp"], p.sems["vp"], p.sems["sp"], p.sems["gp"]

    A = Arena(nc, stack, bytes_per_part=168 * 1024)
    td = [stack.enter_context(
        nc.sbuf_tensor(f"td{i}", [128, S], BF16, is_transpose_dest=True))
        for i in range(8)]
    ps = stack.enter_context(nc.psum_tensor("ps", [128, 4096], F32))
    ps_bf = ps.ap().bitcast(BF16)

    p.junk = A.alloc([128, 8], F32)
    ident_bf = A.alloc([128, 128], BF16)
    ident_f = A.alloc([128, 128], F32)
    tri_f = A.alloc([64, 64], F32)
    dbias = A.alloc([NH, 1], F32)
    alog = A.alloc([NH, 1], F32)
    negA = A.alloc([NH, 1], F32)
    bnw = A.alloc([DS, 1], F32)
    cnw = A.alloc([DS, 1], F32)
    Dbc = A.alloc([128, NH], F32)

    p.op("gpsimd", "memset", ident_bf[:], 0.0)
    p.op("gpsimd", "affine_select", out=ident_bf[:], in_=ident_bf[:],
         compare_op=OP.not_equal, fill=1.0, base=0, pattern=[[-1, 128]],
         channel_multiplier=1)
    p.op("gpsimd", "memset", ident_f[:], 0.0)
    p.op("gpsimd", "affine_select", out=ident_f[:], in_=ident_f[:],
         compare_op=OP.not_equal, fill=1.0, base=0, pattern=[[-1, 128]],
         channel_multiplier=1)
    p.op("gpsimd", "memset", tri_f[:], 1.0)
    p.op("gpsimd", "affine_select", out=tri_f[:], in_=tri_f[:],
         compare_op=OP.is_ge, fill=0.0, base=0, pattern=[[1, 64]],
         channel_multiplier=-1, inc=gp)
    gp_consts = gp.v
    p.dma("sync", dbias[:], dtb_in, qs)
    p.dma("sync", alog[:], alog_in, qs)
    p.dma("sync", bnw[:], bnw_in, qs)
    p.dma("sync", cnw[:], cnw_in, qs)
    p.dma("sync", Dbc[:], d_in.partition_broadcast(128), qs)
    qs_consts = qs.v

    A.mark()
    NT = S // 128
    lnwB = A.alloc([128, D], F32)
    lnbB = A.alloc([128, D], F32)
    u_t = [A.alloc([128, D], F32) for _ in range(2)]
    p.dma("sync", lnwB[:], lnw_in.partition_broadcast(128), qs)
    p.dma("sync", lnbB[:], lnb_in.partition_broadcast(128), qs)
    sq_t = A.alloc([128, D], F32)
    st1 = A.alloc([128, 8], F32)
    sd_t = A.alloc([128, 2], F32)
    xn_t = A.alloc([128, D], F32)
    un_t = A.alloc([128, D], F32)
    unb_t = A.alloc([128, D], BF16)
    xf_t = A.alloc([128, D], F32)

    inv_d = 1.0 / D

    def ln_stats(src, r_out):
        p.wait("scalar", vp, vp.v)
        p.op("scalar", "activation", sq_t[:], src, AF.Square, inc=sp_)
        sv = sp_.v
        p.op("vector", "reduce_sum", st1[:, 0:1], src, axis=AX.X)
        p.wait("vector", sp_, sv)
        p.op("vector", "reduce_sum", st1[:, 1:2], sq_t[:], axis=AX.X)
        p.op("vector", "tensor_scalar_mul", st1[:, 2:3], st1[:, 0:1], inv_d)
        p.op("vector", "tensor_scalar_mul", st1[:, 3:4], st1[:, 1:2], inv_d)
        p.op("vector", "tensor_mul", st1[:, 4:5], st1[:, 2:3], st1[:, 2:3])
        p.op("vector", "tensor_sub", st1[:, 5:6], st1[:, 3:4], st1[:, 4:5])
        p.op("vector", "tensor_scalar_add", st1[:, 6:7], st1[:, 5:6], EPS)
        p.op("vector", "reciprocal", st1[:, 7:8], st1[:, 6:7], inc=vp)
        vv = vp.v
        p.wait("scalar", vp, vv)
        p.op("scalar", "activation", r_out, st1[:, 7:8], AF.Sqrt, inc=sp_)

    for i in range(NT):
        ub = u_t[i % 2]
        p.dma("sync", ub[:], u_in[i * 128:(i + 1) * 128, :], qs)
        p.wait("vector", qs, qs.v)
        p.wait("vector", qg, qg.v)
        p.wait("scalar", qs, qs.v)
        ln_stats(ub[:], sd_t[:, 0:1])
        p.wait("vector", sp_, sp_.v)
        p.op("vector", "tensor_scalar", xn_t[:], ub[:], st1[:, 2:3],
             sd_t[:, 0:1], OP.subtract, OP.mult, inc=vp)
        p.wait("sync", vp, vp.v)
        p.dma("sync", xn_tm[i * 128:(i + 1) * 128, :], xn_t[:], qs)
        ln_stats(xn_t[:], sd_t[:, 1:2])
        p.wait("vector", sp_, sp_.v)
        p.op("vector", "tensor_scalar", un_t[:], xn_t[:], st1[:, 2:3],
             sd_t[:, 1:2], OP.subtract, OP.mult)
        p.op("vector", "tensor_mul", un_t[:], un_t[:], lnwB[:])
        p.op("vector", "tensor_add", unb_t[:], un_t[:], lnbB[:], inc=vp)
        p.wait("gpsimd", vp, vp.v)
        p.dma("gpsimd", un_tm[i * 128:(i + 1) * 128, :], unb_t[:], qg)
        p.wait("tensor", vp, vp.v)
        copy_done = []
        for j in range(16):
            bank = ps[:, (j % 2) * 512:(j % 2) * 512 + 128]
            if j >= 2:
                p.wait("tensor", vp, copy_done[j - 2])
            p.op("tensor", "transpose", bank, xn_t[:, j * 128:(j + 1) * 128],
                 ident_f[:], inc=tp)
            p.wait("vector", tp, tp.v)
            p.op("vector", "tensor_copy", xf_t[:, j * 128:(j + 1) * 128], bank,
                 inc=vp)
            copy_done.append(vp.v)
        p.wait("sync", vp, vp.v)
        p.dma("sync", xn_fm32.rearrange("(c q) t -> q c t", q=128)
              [:, :, i * 128:(i + 1) * 128], xf_t[:].rearrange(
                  "q (c t) -> q c t", c=16), qs)
    p.barrier()
    A.release()
    store_done = {}
    for j in range(16):
        t = td[j % 2]
        if j >= 2:
            p.wait("sync", qs, store_done[j - 2])
        p.dmat("sync", t[:], un_tm[:, j * 128:(j + 1) * 128], qs)
        p.wait("sync", qs, qs.v)
        p.dma("sync", un_fm[j * 128:(j + 1) * 128, :], t[:], qs)
        store_done[j] = qs.v
    p.barrier()

    if stop == "ph0":
        blk = stack.enter_context(nc.Block())
        p.finalize(blk)
        stack.close()
        return nc

    A.mark()
    Bn = A.alloc([DS, S], BF16)
    Cn = A.alloc([DS, S], BF16)
    einv_cm = A.alloc([64, NCH, NH], F32)
    ec_cm = A.alloc([64, NCH, NH], F32)
    ECSb = A.alloc([128, NCH, NH], F32)
    A.mark()
    wbc = A.alloc([128, 16, 320], BF16)
    unf = A.alloc([128, 16, 2048], BF16)
    Bf = A.alloc([128, S], F32)
    Cf = A.alloc([128, S], F32)
    dtf = A.alloc([64, S], F32)
    sqB = A.alloc([128, S], F32)
    rowA = A.alloc([1, S], F32)
    rBb = A.alloc([128, S], F32)
    ones_f = A.alloc([128, 1], F32)
    Mb = rBb[0:64, :]
    daf = Bf[0:64, :]
    srow = dtf[0:1, :]
    erow = Cf[0:1, :]
    dacm = Cf[0:64, :]
    cumacm = sqB[0:64, :]

    p.op("gpsimd", "memset", ones_f[:], 1.0, inc=gp)
    for dc in range(16):
        p.dma("gpsimd", wbc[:, dc, :],
              in_projT[dc * 128:(dc + 1) * 128, 8192:8512], qg)
        p.dma("sync", unf[:, dc, :], un_fm[dc * 128:(dc + 1) * 128, :], qs)
    p.wait("tensor", qg, qg.v)
    p.wait("tensor", qs, qs.v)
    p.wait("tensor", gp, gp.v)

    def proj_part(e0, e1, dst, banksel):
        m = e1 - e0
        evs = []
        for tt in range(4):
            bank = ps[0:m, (banksel * 4 + tt) * 512:(banksel * 4 + tt) * 512 + 512]
            for dc in range(16):
                p.op("tensor", "matmul", bank, wbc[:, dc, e0:e1],
                     unf[:, dc, tt * 512:(tt + 1) * 512],
                     start=(dc == 0), stop=(dc == 15),
                     inc=(tp if dc == 15 else None))
            p.wait("vector", tp, tp.v)
            p.op("vector", "tensor_copy", dst[:, tt * 512:(tt + 1) * 512], bank,
                 inc=vp)
            evs.append(vp.v)
        return evs

    proj_part(0, 128, Bf, 0)
    proj_part(128, 256, Cf, 1)
    p.wait("tensor", vp, vp.v)
    proj_part(256, 320, dtf, 0)

    def rms_norm(src, w_ap, dst_bf, rtrow):
        p.wait("scalar", vp, vp.v)
        p.op("scalar", "activation", sqB[:], src[:], AF.Square, inc=sp_)
        p.wait("tensor", sp_, sp_.v)
        p.wait("tensor", vp, vp.v)
        for tt in range(4):
            bank = ps[0:1, (4 + tt) * 512:(4 + tt) * 512 + 512]
            p.op("tensor", "matmul", bank, ones_f[:],
                 sqB[:, tt * 512:(tt + 1) * 512], start=True, stop=True,
                 inc=tp)
            p.wait("vector", tp, tp.v)
            p.op("vector", "tensor_copy", rowA[:, tt * 512:(tt + 1) * 512],
                 bank, inc=vp)
        p.op("vector", "tensor_scalar", rowA[:], rowA[:], 1.0 / DS, EPS,
             OP.mult, OP.add, inc=vp)
        p.wait("scalar", vp, vp.v)
        p.op("scalar", "activation", rowA[:], rowA[:], AF.Sqrt, inc=sp_)
        p.wait("vector", sp_, sp_.v)
        p.op("vector", "reciprocal", rowA[:], rowA[:], inc=vp)
        p.wait("sync", vp, vp.v)
        p.dma("sync", rt[rtrow:rtrow + 1, :], rowA[:], qs)
        p.wait("sync", qs, qs.v)
        p.dma("sync", rBb[:], rt[rtrow:rtrow + 1, :].partition_broadcast(128),
              qs)
        p.wait("vector", qs, qs.v)
        p.op("vector", "tensor_mul", sqB[:], src[:], rBb[:])
        p.op("vector", "tensor_scalar", dst_bf[:], sqB[:], w_ap, None,
             OP.mult, inc=vp)

    rms_norm(Bf, bnw[:], Bn, 0)
    rms_norm(Cf, cnw[:], Cn, 1)

    p.op("scalar", "activation", negA[:], alog[:], AF.Exp, inc=sp_)
    p.wait("vector", sp_, sp_.v)
    p.op("vector", "tensor_scalar_mul", negA[:], negA[:], -1.0, inc=vp)
    p.wait("scalar", vp, vp.v)
    p.op("scalar", "activation", daf[:], dtf[:], AF.Exp, bias=dbias[:])
    p.op("scalar", "activation", daf[:], daf[:], AF.Ln, bias=1.0, inc=sp_)
    p.wait("vector", sp_, sp_.v)
    p.op("vector", "tensor_scalar", daf[:], daf[:], negA[:], None, OP.mult,
         inc=vp)
    p.wait("tensor", vp, vp.v)
    cdone = []
    for c in range(NCH):
        bank = ps[0:64, (c % 2) * 512:(c % 2) * 512 + 64]
        if c >= 2:
            p.wait("tensor", vp, cdone[c - 2])
        p.op("tensor", "transpose", bank, daf[:, c * 64:(c + 1) * 64],
             ident_f[0:64, 0:64], inc=tp)
        p.wait("vector", tp, tp.v)
        p.op("vector", "tensor_copy", dacm[:, c * 64:(c + 1) * 64], bank,
             inc=vp)
        cdone.append(vp.v)
    p.wait("tensor", vp, vp.v)
    for tt in range(4):
        bank = ps[0:64, (2 + tt) * 512:(2 + tt) * 512 + 512]
        p.op("tensor", "matmul", bank, tri_f[:],
             dacm[:, tt * 512:(tt + 1) * 512], start=True, stop=True, inc=tp)
        p.wait("vector", tp, tp.v)
        p.op("vector", "tensor_copy", cumacm[:, tt * 512:(tt + 1) * 512],
             bank, inc=vp)
    p.wait("sync", vp, vp.v)
    p.dma("sync", rt[3:4, :], cumacm[0:1, :], qs)
    p.dma("sync", rt[4:5, :], cumacm[63:64, :], qs)
    p.wait("sync", qs, qs.v)
    p.dma("sync", rowA[:], rt[3:4, :], qs)
    p.dma("sync", srow[:], rt[4:5, :], qs)
    p.wait("vector", qs, qs.v)
    p.op("vector", "tensor_add", rowA[:], rowA[:], srow[:])
    p.op("vector", "tensor_scalar_mul", rowA[:], rowA[:], 0.5)
    p.op("vector", "tensor_sub", erow[:], srow[:], rowA[:])
    e3 = erow[:].rearrange("p (c h) -> p c h", c=NCH)
    m3 = rowA[:].rearrange("p (c h) -> p c h", c=NCH)
    p.op("vector", "tensor_add", e3[:, 0:NCH - 1, :], e3[:, 0:NCH - 1, :],
         m3[:, 1:NCH, :], inc=vp)
    p.wait("scalar", vp, vp.v)
    p.op("scalar", "activation", erow[:], erow[:], AF.Exp, inc=sp_)
    p.wait("sync", sp_, sp_.v)
    p.dma("sync", rt[2:3, :], erow[:], qs)
    p.dma("sync", rt[5:6, :], rowA[:], qs)
    p.wait("sync", qs, qs.v)
    p.dma("sync", ECSb.rearrange("p a b -> p (a b)"),
          rt[2:3, :].partition_broadcast(128), qs)
    p.dma("sync", Mb[:], rt[5:6, :].partition_broadcast(64), qs)
    p.wait("vector", qs, qs.v)
    p.op("vector", "tensor_sub", daf[:], Mb[:], cumacm[:])
    p.op("vector", "tensor_scalar_min", daf[:], daf[:], 85.0, inc=vp)
    p.wait("scalar", vp, vp.v)
    p.op("scalar", "activation", einv_cm.rearrange("p a b -> p (a b)"),
         daf[:], AF.Exp, inc=sp_)
    p.wait("vector", sp_, sp_.v)
    p.op("vector", "tensor_sub", daf[:], cumacm[:], Mb[:])
    p.op("vector", "tensor_scalar_min", daf[:], daf[:], 85.0, inc=vp)
    p.wait("scalar", vp, vp.v)
    p.op("scalar", "activation", ec_cm.rearrange("p a b -> p (a b)"),
         daf[:], AF.Exp, inc=sp_)
    p.wait("vector", sp_, sp_.v)
    p.wait("sync", sp_, sp_.v)

    if "bn_tap" in taps:
        bn_tap = dram("bn_tap", [DS, S], BF16)
        cn_tap = dram("cn_tap", [DS, S], BF16)
        cuma_tap = dram("cuma_tap", [64, S], F32)
        p.wait("sync", vp, vp.v)
        p.dma("sync", bn_tap, Bn[:], qs)
        p.dma("sync", cn_tap, Cn[:], qs)
        p.dma("sync", cuma_tap, cumacm[:], qs)

    p.barrier()
    A.release()

    if stop == "ph1":
        blk = stack.enter_context(nc.Block())
        p.finalize(blk)
        stack.close()
        return nc

    for q in range(4):
        A.mark()
        wx = A.alloc([128, 16, 1024], BF16)
        wz = A.alloc([128, 16, 1024], BF16)
        unp = [A.alloc([128, 16, 128], BF16) for _ in range(2)]
        xpair = [A.alloc([128, 1024], BF16) for _ in range(2)]
        szpair = [A.alloc([128, 1024], BF16) for _ in range(2)]
        xsh = A.alloc([64, 1024], BF16)
        szsh = A.alloc([64, 1024], BF16)
        xt = A.alloc([64, 1024], BF16)
        btb = A.alloc([64, 128], BF16)
        mmsk = A.alloc([64, 64], BF16)
        stA = A.alloc([128, 1024], BF16)
        stB = A.alloc([128, 1024], BF16)
        sttmp = A.alloc([128, 1024], BF16)
        t1 = A.alloc([64, 1024], F32)
        t2 = A.alloc([64, 1024], F32)
        ygb = A.alloc([64, 1024], BF16)

        EX = 4096 + 1024 * q
        EZ = 1024 * q
        H0 = 16 * q

        for dc in range(16):
            p.dma("gpsimd", wx[:, dc, :],
                  in_projT[dc * 128:(dc + 1) * 128, EX:EX + 1024], qg)
        wx_ready = qg.v
        for dc in range(16):
            p.dma("gpsimd", wz[:, dc, :],
                  in_projT[dc * 128:(dc + 1) * 128, EZ:EZ + 1024], qg)
        wz_ready = qg.v
        p.op("vector", "memset", stA[:], 0.0, inc=vp)

        ps_sc = ps[0:64, 3072:3136]
        ps_btv = ps_bf[0:64, 6400:6528]
        xz_free = sp_.v
        y_free = vp.v
        p_free = vp.v
        sc_free = vp.v
        st_upd = vp.v
        un_rdy = {}
        z_done = {}
        yg_dma = 0
        for pr in range(16):
            up = unp[pr % 2]
            if pr >= 2:
                p.wait("sync", tp, z_done[pr - 2])
            un_rdy[pr] = p.dma(
                "sync", up[:],
                un_fm.rearrange("(c pp) t -> pp c t", pp=128)
                [:, :, pr * 128:(pr + 1) * 128], qs)
            p.wait("tensor", qs, un_rdy[pr])
            if pr == 0:
                p.wait("tensor", qg, wx_ready)
            p.wait("tensor", sp_, xz_free)
            for et in range(2):
                bank = ps[:, et * 512:et * 512 + 512]
                for dc in range(16):
                    p.op("tensor", "matmul", bank, up[:, dc, :],
                         wx[:, dc, et * 512:et * 512 + 512],
                         start=(dc == 0), stop=(dc == 15),
                         inc=(tp if dc == 15 else None))
            p.wait("vector", tp, tp.v)
            p.wait("vector", qg, qg.v)
            p.op("vector", "tensor_copy", xpair[pr % 2][:], ps[:, 0:1024],
                 inc=vp)
            x_ev = vp.v
            p.wait("tensor", vp, x_ev)
            if pr == 0:
                p.wait("tensor", qg, wz_ready)
            for et in range(2):
                bank = ps[:, et * 512:et * 512 + 512]
                for dc in range(16):
                    p.op("tensor", "matmul", bank, up[:, dc, :],
                         wz[:, dc, et * 512:et * 512 + 512],
                         start=(dc == 0), stop=(dc == 15),
                         inc=(tp if dc == 15 else None))
            z_done[pr] = tp.v
            p.wait("scalar", tp, tp.v)
            p.wait("scalar", qg, qg.v)
            p.op("scalar", "activation", szpair[pr % 2][:], ps[:, 0:1024],
                 AF.Silu, inc=sp_)
            sz_ev = xz_free = sp_.v

            for ci in range(2):
                c = 2 * pr + ci
                if ci == 0:
                    xc, szc = xpair[pr % 2][0:64, :], szpair[pr % 2][0:64, :]
                    xc_rdy, sz_rdy = x_ev, sz_ev
                else:
                    p.wait("gpsimd", vp, x_ev)
                    p.wait("gpsimd", sp_, sz_ev)
                    p.dma("gpsimd", xsh[:], xpair[pr % 2][64:128, :], qg)
                    p.dma("gpsimd", szsh[:], szpair[pr % 2][64:128, :], qg)
                    xc, szc = xsh, szsh
                    xc_rdy, sz_rdy = None, None
                    sh_rdy = qg.v
                p.wait("tensor", vp, sc_free)
                p.op("tensor", "matmul", ps_sc, Bn[:, c * 64:(c + 1) * 64],
                     Cn[:, c * 64:(c + 1) * 64], start=True, stop=True)
                p.op("tensor", "transpose", ps_btv, Bn[:, c * 64:(c + 1) * 64],
                     ident_bf[:], inc=tp)
                p.wait("vector", tp, tp.v)
                if ci == 1:
                    p.wait("vector", qg, sh_rdy)
                p.op("vector", "tensor_mul", mmsk[:], ps_sc, tri_f[:])
                p.op("vector", "tensor_copy", btb[:], ps_btv)
                p.op("vector", "tensor_tensor",
                     xt[:].rearrange("p (h j) -> p h j", h=16),
                     xc.rearrange("p (h j) -> p h j", h=16),
                     einv_cm[:, c, H0:H0 + 16].unsqueeze(2)
                     .broadcast_to([64, 16, 64]), OP.mult, inc=vp)
                sc_free = pre_y = vp.v
                p.wait("tensor", vp, pre_y)
                stcur = stA if c % 2 == 0 else stB
                stnew = stB if c % 2 == 0 else stA
                for nt in range(2):
                    ybank = ps[0:64, 1024 + nt * 512:1024 + nt * 512 + 512]
                    p.op("tensor", "matmul", ybank, mmsk[:],
                         xt[:, nt * 512:nt * 512 + 512],
                         start=True, stop=False)
                    p.op("tensor", "matmul", ybank,
                         Cn[:, c * 64:(c + 1) * 64],
                         stcur[:, nt * 512:nt * 512 + 512],
                         start=False, stop=True,
                         inc=(tp if nt == 1 else None))
                y_mm = tp.v
                for nt in range(2):
                    pbank = ps[:, 2048 + nt * 512:2048 + nt * 512 + 512]
                    p.op("tensor", "matmul", pbank, btb[:],
                         xt[:, nt * 512:nt * 512 + 512],
                         start=True, stop=True,
                         inc=(tp if nt == 1 else None))
                p_mm = tp.v
                p.wait("vector", tp, y_mm)
                p.op("vector", "tensor_tensor",
                     t1[:].rearrange("p (h j) -> p h j", h=16),
                     ps[0:64, 1024:2048].rearrange("p (h j) -> p h j", h=16),
                     ec_cm[:, c, H0:H0 + 16].unsqueeze(2)
                     .broadcast_to([64, 16, 64]), OP.mult)
                p.op("vector", "tensor_tensor",
                     t2[:].rearrange("p (h j) -> p h j", h=16),
                     xc.rearrange("p (h j) -> p h j", h=16),
                     Dbc[0:64, H0:H0 + 16].unsqueeze(2)
                     .broadcast_to([64, 16, 64]), OP.mult)
                p.op("vector", "tensor_add", t1[:], t1[:], t2[:])
                if ci == 0:
                    p.wait("vector", sp_, sz_ev)
                p.wait("vector", qs, yg_dma)
                p.op("vector", "tensor_mul", ygb[:], t1[:], szc, inc=vp)
                y_free = vp.v
                p.wait("sync", vp, y_free)
                yg_dma = p.dma("sync", yg_tm[c * 64:(c + 1) * 64,
                                             1024 * q:1024 * (q + 1)],
                               ygb[:], qs)
                p.wait("vector", tp, p_mm)
                p.op("vector", "tensor_tensor", sttmp[:], ps[:, 2048:3072],
                     stcur[:], OP.add)
                p.op("vector", "tensor_tensor",
                     stnew[:].rearrange("p (h j) -> p h j", h=16),
                     sttmp[:].rearrange("p (h j) -> p h j", h=16),
                     ECSb[:, c, H0:H0 + 16].unsqueeze(2)
                     .broadcast_to([128, 16, 64]), OP.mult, inc=vp)
                st_upd = vp.v
                p_free = vp.v
        p.barrier()
        A.release()
    A.release()

    if stop == "ph2":
        blk = stack.enter_context(nc.Block())
        p.finalize(blk)
        stack.close()
        return nc

    A.mark()
    wob2 = [A.alloc([128, 4, 2048], BF16) for _ in range(2)]
    stage = [A.alloc([128, 2048], F32) for _ in range(2)]
    u_t3 = [A.alloc([128, 2048], F32) for _ in range(2)]
    xn_t3 = [A.alloc([128, 2048], F32) for _ in range(2)]
    NG3 = 8
    ygf_rdy = {}
    wob_rdy = {}
    mm3_done = {}
    ev_done = {}

    def ph3_prefetch(g):
        if g >= NG3:
            return
        if g >= 2:
            p.wait("sync", tp, mm3_done.get(g - 2, 0))
            p.wait("gpsimd", tp, mm3_done.get(g - 2, 0))
        base = g * 4
        for ec in range(4):
            gc = base + ec
            ygf_rdy[g] = p.dmat("sync", td[(g % 2) * 4 + ec][:],
                                yg_tm[:, gc * 128:(gc + 1) * 128], qs)
            wob_rdy[g] = p.dma("gpsimd", wob2[g % 2][:, ec, :],
                               woutT[gc * 128:(gc + 1) * 128, :], qg)

    ph3_prefetch(0)
    out_w = 0
    for grp in range(NG3):
        ph3_prefetch(grp + 1)
        tds = td[(grp % 2) * 4:(grp % 2) * 4 + 4]
        wob = wob2[grp % 2]
        p.wait("tensor", qs, ygf_rdy[grp])
        p.wait("tensor", qg, wob_rdy[grp])
        if grp >= 1:
            p.wait("gpsimd", qg, out_w)
        for tc in range(16):
            if grp == 0:
                if tc >= 2:
                    p.wait("sync", vp, ev_done[(tc - 2, 3)])
                p.dma("sync", u_t3[tc % 2][:],
                      u_in[tc * 128:(tc + 1) * 128, :], qs)
                p.dma("sync", xn_t3[tc % 2][:],
                      xn_tm[tc * 128:(tc + 1) * 128, :], qs)
            ld_rdy = qs.v
            for dt_ in range(4):
                bank = ps[:, (dt_ % 2) * 512:(dt_ % 2) * 512 + 512]
                prev = (tc, dt_ - 2) if dt_ >= 2 else (tc - 1, dt_ + 2)
                if prev in ev_done:
                    p.wait("tensor", vp, ev_done[prev])
                for ec in range(4):
                    p.op("tensor", "matmul", bank,
                         tds[ec][:, tc * 128:(tc + 1) * 128],
                         wob[:, ec, dt_ * 512:(dt_ + 1) * 512],
                         start=(ec == 0), stop=(ec == 3),
                         inc=(tp if ec == 3 else None))
                p.wait("vector", tp, tp.v)
                if dt_ == 0:
                    p.wait("vector", qg, qg.v)
                    if grp == 0:
                        p.wait("vector", qs, ld_rdy)
                sg = stage[tc % 2][:, dt_ * 512:(dt_ + 1) * 512]
                if grp == 0:
                    p.op("vector", "tensor_add", sg, bank,
                         u_t3[tc % 2][:, dt_ * 512:(dt_ + 1) * 512])
                    p.op("vector", "tensor_add", sg, sg,
                         xn_t3[tc % 2][:, dt_ * 512:(dt_ + 1) * 512],
                         inc=vp)
                else:
                    p.op("vector", "tensor_copy", sg, bank, inc=vp)
                ev_done[(tc, dt_)] = vp.v
            p.wait("gpsimd", vp, vp.v)
            out_w = p.dma("gpsimd", out[tc * 128:(tc + 1) * 128, :],
                          stage[tc % 2][:], qg,
                          **({} if grp == 0 else {"accum_op": OP.add}))
        mm3_done[grp] = tp.v
    p.barrier()
    A.release()

    if stop == "ph3":
        blk = stack.enter_context(nc.Block())
        p.finalize(blk)
        stack.close()
        return nc

    A.mark()
    xcol = [A.alloc([128, 16, 128], F32) for _ in range(2)]
    gate_sb = A.alloc([128, 16, 8], F32)
    sgt_sb = A.alloc([128, 16], F32)
    lg = A.alloc([128, 16, 8], F32)
    ro = A.alloc([128, 16, 8], F32)
    rank = A.alloc([128, 16, 8], F32)
    tmp4 = A.alloc([128, 16, 8], F32)
    msk4 = A.alloc([128, 16, 8], F32)
    mx4 = A.alloc([128, 16, 1], F32)
    io8 = A.alloc([128, 8], F32)
    io8i = A.alloc([128, 8], mybir.dt.int32)
    tb = A.alloc([128, 7, 8], F32)
    sgrow = A.alloc([1, S], F32)
    combf = A.alloc([8, S], F32)

    p.dma("sync", gate_sb[:],
          gateT.rearrange("(c p) e -> p c e", p=128), qs)
    p.dma("sync", sgt_sb[:], sgT, qs)
    p.op("gpsimd", "iota", io8i[:], pattern=[[1, 8]], base=0,
         channel_multiplier=0, inc=gp)
    p.wait("vector", gp, gp.v)
    p.op("vector", "tensor_copy", io8[:], io8i[:])
    for s_ in range(1, 8):
        p.op("vector", "tensor_single_scalar", tb[:, s_ - 1, :], io8[:],
             float(s_), OP.is_ge)
    ld_prev = 0
    for tc in range(16):
        xc4 = xcol[tc % 2]
        if tc >= 2:
            p.wait("sync", tp, ld_prev)
        p.dma("sync", xc4[:],
              xn_fm32.rearrange("(c p) t -> p c t", p=128)
              [:, :, tc * 128:(tc + 1) * 128], qs)
        p.wait("tensor", qs, qs.v)
        bl = ps[:, 0:8]
        bs = ps[0:1, 512:640]
        for dc in range(16):
            p.op("tensor", "matmul", bl, xc4[:, dc, :], gate_sb[:, dc, :],
                 start=(dc == 0), stop=(dc == 15),
                 inc=(tp if dc == 15 else None))
        for dc in range(16):
            p.op("tensor", "matmul", bs, sgt_sb[:, dc:dc + 1], xc4[:, dc, :],
                 start=(dc == 0), stop=(dc == 15),
                 inc=(tp if dc == 15 else None))
        ld_prev = tp.v
        p.wait("vector", tp, tp.v)
        p.op("vector", "tensor_copy", lg[:, tc, :], bl)
        p.op("vector", "tensor_copy", sgrow[:, tc * 128:(tc + 1) * 128], bs,
             inc=vp)
        p.wait("tensor", vp, vp.v)
    p.op("vector", "memset", rank[:], 0.0)
    for s_ in range(1, 8):
        p.op("vector", "tensor_copy", ro[:, :, s_:8], lg[:, :, 0:8 - s_])
        p.op("vector", "tensor_copy", ro[:, :, 0:s_], lg[:, :, 8 - s_:8])
        p.op("vector", "tensor_tensor", tmp4[:], ro[:], lg[:], OP.is_gt)
        p.op("vector", "tensor_add", rank[:], rank[:], tmp4[:])
        p.op("vector", "tensor_tensor", tmp4[:], ro[:], lg[:], OP.is_equal)
        p.op("vector", "tensor_mul", tmp4[:], tmp4[:],
             tb[:, s_ - 1:s_, :].broadcast_to([128, 16, 8]))
        p.op("vector", "tensor_add", rank[:], rank[:], tmp4[:])
    p.op("vector", "tensor_single_scalar", msk4[:], rank[:], 3.0, OP.is_le)
    p.op("vector", "reduce_max", mx4[:], lg[:], axis=AX.X)
    p.op("vector", "tensor_sub", tmp4[:], lg[:],
         mx4[:].broadcast_to([128, 16, 8]), inc=vp)
    p.wait("scalar", vp, vp.v)
    p.op("scalar", "activation", tmp4[:], tmp4[:], AF.Exp, inc=sp_)
    p.wait("vector", sp_, sp_.v)
    p.op("vector", "tensor_mul", tmp4[:], tmp4[:], msk4[:])
    p.op("vector", "reduce_sum", mx4[:], tmp4[:], axis=AX.X)
    p.op("vector", "reciprocal", mx4[:], mx4[:])
    p.op("vector", "tensor_mul", tmp4[:], tmp4[:],
         mx4[:].broadcast_to([128, 16, 8]), inc=vp)
    p.wait("tensor", vp, vp.v)
    cdone = []
    for tc in range(16):
        bank = ps[0:8, (tc % 2) * 512:(tc % 2) * 512 + 128]
        if tc >= 2:
            p.wait("tensor", vp, cdone[tc - 2])
        p.op("tensor", "transpose", bank, tmp4[:, tc, :], ident_f[:], inc=tp)
        p.wait("vector", tp, tp.v)
        p.op("vector", "tensor_copy", combf[:, tc * 128:(tc + 1) * 128], bank,
             inc=vp)
        cdone.append(vp.v)
    p.wait("sync", vp, vp.v)
    p.dma("sync", comb_fm[0:8, :], combf[:], qs)
    p.op("scalar", "activation", sgrow[:], sgrow[:], AF.Sigmoid, inc=sp_)
    p.wait("sync", sp_, sp_.v)
    for g in range(4):
        p.dma("sync", comb_fm[8 + g:9 + g, :], sgrow[:], qs)
    p.barrier()
    A.release()

    if stop == "ph4":
        blk = stack.enter_context(nc.Block())
        p.finalize(blk)
        stack.close()
        return nc

    A.mark()
    xnb = A.alloc([128, 16, 2048], BF16)
    actk = A.alloc([128, 11, 2048], BF16)
    wdk = A.alloc([128, 11, 2048], BF16)
    combB = A.alloc([128, 2048], BF16)
    t5sg = [A.alloc([128, 512], F32) for _ in range(2)]
    for dc in range(16):
        p.dma("gpsimd", xnb[:, dc, :], xn_fm32[dc * 128:(dc + 1) * 128, :], qg)
    xnb_rdy = qg.v
    wgc = [td[0].ap().rearrange("p (c i) -> p c i", c=16),
           td[1].ap().rearrange("p (c i) -> p c i", c=16)]
    wuc = [td[2].ap().rearrange("p (c i) -> p c i", c=16),
           td[3].ap().rearrange("p (c i) -> p c i", c=16)]
    dsA = [td[4].ap().bitcast(F32), td[6].ap().bitcast(F32)]
    dsB = [td[5].ap().bitcast(F32), td[7].ap().bitcast(F32)]

    def exp_srcs(k, ic):
        if k < 8:
            return (wgT[k][:, ic * 128:(ic + 1) * 128],
                    wuT[k][:, ic * 128:(ic + 1) * 128],
                    wdT[k][ic * 128:(ic + 1) * 128, :])
        g = k - 8
        o = g * I_EXP + ic * 128
        return (swgT[:, o:o + 128], swuT[:, o:o + 128], swdT[o:o + 128, :])

    wg_s, wu_s, _ = exp_srcs(0, 0)
    p.dma("gpsimd", wgc[0][:], wg_s.rearrange("(c p) i -> p c i", p=128), qg)
    p.dma("gpsimd", wuc[0][:], wu_s.rearrange("(c p) i -> p c i", p=128), qg)
    w_rdy = {(0, 0): qg.v}

    prev_accum = 0
    up_ev_prev = 0
    buf_last_use = {0: 0, 1: 0}
    slice_ev = {}
    up_bank_ev = []
    for k in range(NEXP):
        p.wait("gpsimd", vp, up_ev_prev)
        comb_rdy = p.dma("gpsimd", combB[:],
                         comb_fm[k:k + 1, :].partition_broadcast(128), qg)
        wd_rdy = []
        for ic in range(11):
            _, _, wd_s = exp_srcs(k, ic)
            wd_rdy.append(p.dma("gpsimd", wdk[:, ic, :], wd_s, qg))
        for ic in range(11):
            nk, nic = (k, ic + 1) if ic < 10 else (k + 1, 0)
            if nk < NEXP:
                par = (ic + 1) % 2
                p.wait("gpsimd", tp, buf_last_use[par])
                wg_s, wu_s, _ = exp_srcs(nk, nic)
                p.dma("gpsimd", wgc[par][:],
                      wg_s.rearrange("(c p) i -> p c i", p=128), qg)
                p.dma("gpsimd", wuc[par][:],
                      wu_s.rearrange("(c p) i -> p c i", p=128), qg)
                w_rdy[(nk, nic)] = qg.v
            p.wait("tensor", qg, w_rdy[(k, ic)])
            if k == 0 and ic == 0:
                p.wait("tensor", qg, xnb_rdy)
            for tt in range(4):
                bg = ps[:, (tt % 2) * 1024:(tt % 2) * 1024 + 512]
                bu = ps[:, (tt % 2) * 1024 + 512:(tt % 2) * 1024 + 1024]
                if len(up_bank_ev) >= 2:
                    p.wait("tensor", vp, up_bank_ev[-2])
                for dc in range(16):
                    p.op("tensor", "matmul", bg, wgc[ic % 2][:, dc, :],
                         xnb[:, dc, tt * 512:(tt + 1) * 512],
                         start=(dc == 0), stop=(dc == 15))
                for dc in range(16):
                    p.op("tensor", "matmul", bu, wuc[ic % 2][:, dc, :],
                         xnb[:, dc, tt * 512:(tt + 1) * 512],
                         start=(dc == 0), stop=(dc == 15),
                         inc=(tp if dc == 15 else None))
                buf_last_use[ic % 2] = tp.v
                p.wait("scalar", tp, tp.v)
                p.wait("scalar", vp, slice_ev.get(tt % 2, 0))
                sgb = t5sg[tt % 2]
                p.op("scalar", "activation", sgb[:], bg, AF.Silu, inc=sp_)
                p.wait("vector", sp_, sp_.v)
                if ic == 0 and tt == 0:
                    p.wait("vector", qg, comb_rdy)
                sl = slice(tt * 512, (tt + 1) * 512)
                p.op("vector", "tensor_mul", sgb[:], sgb[:], bu)
                p.op("vector", "tensor_mul", actk[:, ic, sl], sgb[:],
                     combB[:, sl], inc=vp)
                slice_ev[tt % 2] = vp.v
                up_bank_ev.append(vp.v)
        up_ev_prev = vp.v
        p.wait("tensor", qg, wd_rdy[-1])
        p.wait("tensor", vp, up_ev_prev)
        if k > 0:
            p.wait("gpsimd", qg, prev_accum)
        ds_dma = {}
        dn_ev = {}
        for tc in range(16):
            for dt_ in range(4):
                bank = ps[:, 2048 + (dt_ % 2) * 512:2048 + (dt_ % 2) * 512 + 512]
                prev = (tc, dt_ - 2) if dt_ >= 2 else (tc - 1, dt_ + 2)
                if prev in dn_ev:
                    p.wait("tensor", vp, dn_ev[prev])
                for ic in range(11):
                    p.op("tensor", "matmul", bank,
                         actk[:, ic, tc * 128:(tc + 1) * 128],
                         wdk[:, ic, dt_ * 512:(dt_ + 1) * 512],
                         start=(ic == 0), stop=(ic == 10),
                         inc=(tp if ic == 10 else None))
                p.wait("vector", tp, tp.v)
                if dt_ == 0 and tc >= 2:
                    p.wait("vector", qg, ds_dma[tc - 2])
                half = dsA[tc % 2] if dt_ < 2 else dsB[tc % 2]
                p.op("vector", "tensor_copy",
                     half[:, (dt_ % 2) * 512:(dt_ % 2) * 512 + 512], bank,
                     inc=vp)
                dn_ev[(tc, dt_)] = vp.v
            p.wait("gpsimd", vp, vp.v)
            p.dma("gpsimd", out[tc * 128:(tc + 1) * 128, 0:1024],
                  dsA[tc % 2][:], qg, accum_op=OP.add)
            ds_dma[tc] = p.dma("gpsimd",
                               out[tc * 128:(tc + 1) * 128, 1024:2048],
                               dsB[tc % 2][:], qg, accum_op=OP.add)
        prev_accum = qg.v
    p.barrier()
    A.release()

    blk = stack.enter_context(nc.Block())
    p.finalize(blk)
    stack.close()
    return nc



def kernel(**inputs):
    w = inputs
    c = np.ascontiguousarray
    in_projT = c(w["L0_attn_in_proj_w"].T)
    woutT = c(w["L0_attn_out_proj_w"].T)
    gateT = c(w["L0_ffn_gate_w"].T)
    sgT = c(w["L0_ffn_shared_gate_w"].reshape(16, 128).T)
    wgT = c(w["L0_ffn_wg"].transpose(0, 2, 1))
    wuT = c(w["L0_ffn_wu"].transpose(0, 2, 1))
    wdT = c(w["L0_ffn_wd"].transpose(0, 2, 1))
    swgT = c(w["L0_ffn_swg"].T)
    swuT = c(w["L0_ffn_swu"].T)
    swdT = c(w["L0_ffn_swd"].T)
    base = {
        "in_projT": in_projT, "woutT": woutT, "gateT": gateT, "sgT": sgT,
        "wgT": wgT, "wuT": wuT, "wdT": wdT, "swgT": swgT, "swuT": swuT,
        "swdT": swdT,
        "Dh": c(w["L0_attn_D"].reshape(1, NH)),
        "dt_bias": c(w["L0_attn_dt_bias"].reshape(NH, 1)),
        "A_log": c(w["L0_attn_A_log"].reshape(NH, 1)),
        "bnorm_w": c(w["L0_attn_bnorm_w"].reshape(DS, 1)),
        "cnorm_w": c(w["L0_attn_cnorm_w"].reshape(DS, 1)),
        "ln_w": c(w["L0_attn_ln_w"].reshape(1, D)),
        "ln_b": c(w["L0_attn_ln_b"].reshape(1, D)),
    }
    u = w["L0_attn_u"]
    in_maps = []
    for b in range(8):
        m = dict(base)
        m["u"] = c(u[b])
        in_maps.append(m)
    nc = build_program(taps=DEV_TAPS, stop=DEV_STOP)
    res = run_bass_kernel_spmd(nc, in_maps, core_ids=list(range(8)))
    if DEV_TAPS:
        kernel.dev_results = res.results
    return np.stack([r["out"] for r in res.results], axis=0)


# version 05
# speedup vs baseline: 1.1505x; 1.1505x over previous
import math
import numpy as np
import ml_dtypes
import concourse.bass as bass
import concourse.mybir as mybir
from concourse.bass_utils import run_bass_kernel_spmd

F32 = mybir.dt.float32
BF16 = mybir.dt.bfloat16
OP = mybir.AluOpType
AF = mybir.ActivationFunctionType
AX = mybir.AxisListType

S = 2048
D = 2048
NH = 64
HD = 64
DS = 128
CH = 64
NCH = S // CH
DIN = 4096
DPROJ = 8512
NEXP = 12
I_EXP = 1408
EPS = 1e-5

DEV_TAPS = ()
DEV_STOP = None

ENGINES = ("sync", "scalar", "vector", "tensor", "gpsimd")


class Sem:
    def __init__(self, h):
        self.h = h
        self.v = 0


class P:

    def __init__(self, nc, stack):
        self.nc = nc
        self.stack = stack
        self.streams = {e: [] for e in ENGINES}
        self.sems = {}
        self.barv = 0
        self._uid = 0
        self.HAZ_WIN = 2
        self.recent = {e: [] for e in ENGINES}
        self.junk = None

    @staticmethod
    def _apkey(ap):
        try:
            esz = {F32: 4, BF16: 2, mybir.dt.uint8: 1}.get(ap.dtype, 4)
            b0 = ap.offset * esz
            if ap.ndim > 2:
                return (ap.name, b0, b0 + (1 << 40))
            return (ap.name, b0, b0 + ap.free_nbytes())
        except Exception:
            return None

    @staticmethod
    def _overlap(k1, k2):
        return (k1 is not None and k2 is not None and k1[0] == k2[0]
                and k1[1] < k2[2] and k2[1] < k1[2])

    def _spacer(self, eng):
        j = self.junk
        if eng == "vector":
            self.emit(eng, lambda e: e.tensor_copy(j[:, 0:4], j[:, 4:8]))
        elif eng == "scalar":
            self.emit(eng, lambda e: e.copy(j[:, 0:4], j[:, 4:8]))
        elif eng == "gpsimd":
            self.emit(eng, lambda e: e.tensor_copy(j[:, 0:4], j[:, 4:8]))
        else:
            self.emit(eng, lambda e: e.nop())
        self.recent[eng].append(None)
        del self.recent[eng][:-self.HAZ_WIN]

    def sem(self, name):
        s = Sem(self.stack.enter_context(self.nc.semaphore(name)))
        self.sems[name] = s
        return s

    def emit(self, eng, fn):
        self.streams[eng].append(fn)

    def wait(self, eng, sem, val=None):
        v = sem.v if val is None else val
        self.emit(eng, lambda e, h=sem.h, v=v: e.wait_ge(h, v))

    def dma(self, q, out, in_, sem, inc=True, **kw):
        def fn(e, out=out, in_=in_, kw=kw, h=sem.h):
            e.dma_start(out, in_, **kw).then_inc(h, 16)
        self.emit(q, fn)
        sem.v += 16
        return sem.v

    def dmat(self, q, out, in_, sem):
        def fn(e, out=out, in_=in_, h=sem.h):
            e.dma_start(out, in_, transpose=True).then_inc(h, 16)
        self.emit(q, fn)
        sem.v += 16
        return sem.v

    def op(self, eng, method, *args, inc=None, **kw):
        aps = [a for a in args if isinstance(a, bass.AP)]
        kw_out = kw.get("out")
        if kw_out is not None:
            wr = kw_out
            rds = aps + [v for k, v in kw.items()
                         if k != "out" and isinstance(v, bass.AP)]
        else:
            wr = aps[0] if aps else None
            rds = aps[1:] + [v for v in kw.values() if isinstance(v, bass.AP)]
        rkeys = [self._apkey(a) for a in rds]
        guard = 0
        while any(self._overlap(rk, wk) for rk in rkeys
                  for wk in self.recent[eng]):
            self._spacer(eng)
            guard += 1
            assert guard <= self.HAZ_WIN + 1
        if inc is None:
            def fn(e, m=method, a=args, kw=kw):
                getattr(e, m)(*a, **kw)
        else:
            inc.v += 1

            def fn(e, m=method, a=args, kw=kw, h=inc.h):
                getattr(e, m)(*a, **kw).then_inc(h, 1)
        self.emit(eng, fn)
        self.recent[eng].append(self._apkey(wr))
        del self.recent[eng][:-self.HAZ_WIN]
        return inc.v if inc is not None else None

    def barrier(self):
        bar = self.sems["bar"]
        for eng in ENGINES:
            if eng in ("sync", "gpsimd", "scalar"):
                qs = self.sems["q_" + eng]
                self.wait(eng, qs, qs.v)
            self.emit(eng, lambda e, h=bar.h: e.nop().then_inc(h, 1))
        self.barv += 5
        for eng in ENGINES:
            self.wait(eng, bar, self.barv)

    def finalize(self, block):
        for eng in ENGINES:
            stream = self.streams[eng]
            getattr(block, eng)(lambda e, stream=stream: [f(e) for f in stream])


class Arena:

    def __init__(self, nc, stack, bytes_per_part=200 * 1024):
        self.cap = bytes_per_part
        self.t = stack.enter_context(
            nc.sbuf_tensor("arena", [128, bytes_per_part], mybir.dt.uint8))
        self.off = 0
        self.marks = []

    def alloc(self, shape, dt):
        esz = {F32: 4, BF16: 2, mybir.dt.int32: 4}[dt]
        free = int(np.prod(shape[1:]))
        nbytes = free * esz
        off = (self.off + 511) & ~511
        assert off + nbytes <= self.cap, f"SBUF arena overflow: {off + nbytes}"
        self.off = off + nbytes
        v = self.t[0:shape[0], off:off + nbytes].bitcast(dt)
        if len(shape) == 2:
            return v
        pat = "p (" + " ".join(f"d{i}" for i in range(1, len(shape))) + ") -> p " + \
              " ".join(f"d{i}" for i in range(1, len(shape)))
        return v.rearrange(pat, **{f"d{i}": shape[i] for i in range(1, len(shape) - 1)})

    def mark(self):
        self.marks.append(self.off)

    def release(self):
        self.off = self.marks.pop()


def ceil_div(a, b):
    return (a + b - 1) // b



def build_program(taps=(), stop=None):
    from contextlib import ExitStack
    stack = ExitStack()
    nc = bass.Bass("TRN2", target_bir_lowering=False)

    def dram(name, shape, dt, kind="Internal"):
        k = "ExternalOutput" if (name in taps and kind == "Internal") else kind
        return nc.dram_tensor(name, shape, dt, kind=k).ap()

    u_in = dram("u", [S, D], F32, "ExternalInput")
    in_projT = dram("in_projT", [D, DPROJ], F32, "ExternalInput")
    woutT = dram("woutT", [DIN, D], F32, "ExternalInput")
    gateT = dram("gateT", [D, 8], F32, "ExternalInput")
    sgT = dram("sgT", [128, 16], F32, "ExternalInput")
    wgT = dram("wgT", [8, D, I_EXP], F32, "ExternalInput")
    wuT = dram("wuT", [8, D, I_EXP], F32, "ExternalInput")
    wdT = dram("wdT", [8, I_EXP, D], F32, "ExternalInput")
    swgT = dram("swgT", [D, 4 * I_EXP], F32, "ExternalInput")
    swuT = dram("swuT", [D, 4 * I_EXP], F32, "ExternalInput")
    swdT = dram("swdT", [4 * I_EXP, D], F32, "ExternalInput")
    d_in = dram("Dh", [1, NH], F32, "ExternalInput")
    dtb_in = dram("dt_bias", [NH, 1], F32, "ExternalInput")
    alog_in = dram("A_log", [NH, 1], F32, "ExternalInput")
    bnw_in = dram("bnorm_w", [DS, 1], F32, "ExternalInput")
    cnw_in = dram("cnorm_w", [DS, 1], F32, "ExternalInput")
    lnw_in = dram("ln_w", [1, D], F32, "ExternalInput")
    lnb_in = dram("ln_b", [1, D], F32, "ExternalInput")

    out = dram("out", [S, D], F32, "ExternalOutput")

    xn_tm = dram("xn_tm", [S, D], F32)
    xn_fm32 = dram("xn_fm32", [D, S], F32)
    un_tm = dram("un_tm", [S, D], BF16)
    un_fm = dram("un_fm", [D, S], BF16)
    yg_tm = dram("yg_tm", [S, DIN], BF16)
    comb_fm = dram("comb_fm", [NEXP, S], F32)
    rt = dram("rt", [64, S], F32)

    p = P(nc, stack)
    for name in ("q_sync", "q_gpsimd", "q_scalar", "tp", "vp", "sp", "gp", "bar"):
        p.sem(name)
    qs, qg = p.sems["q_sync"], p.sems["q_gpsimd"]
    tp, vp, sp_, gp = p.sems["tp"], p.sems["vp"], p.sems["sp"], p.sems["gp"]

    A = Arena(nc, stack, bytes_per_part=198 * 1024)
    ps = stack.enter_context(nc.psum_tensor("ps", [128, 4096], F32))
    ps_bf = ps.ap().bitcast(BF16)

    p.junk = A.alloc([128, 8], F32)
    ident_bf = A.alloc([128, 128], BF16)
    ident_f = A.alloc([128, 128], F32)
    tri_f = A.alloc([64, 64], F32)
    dbias = A.alloc([NH, 1], F32)
    alog = A.alloc([NH, 1], F32)
    negA = A.alloc([NH, 1], F32)
    bnw = A.alloc([DS, 1], F32)
    cnw = A.alloc([DS, 1], F32)
    Dbc = A.alloc([128, NH], F32)

    p.op("gpsimd", "memset", ident_bf[:], 0.0)
    p.op("gpsimd", "affine_select", out=ident_bf[:], in_=ident_bf[:],
         compare_op=OP.not_equal, fill=1.0, base=0, pattern=[[-1, 128]],
         channel_multiplier=1)
    p.op("gpsimd", "memset", ident_f[:], 0.0)
    p.op("gpsimd", "affine_select", out=ident_f[:], in_=ident_f[:],
         compare_op=OP.not_equal, fill=1.0, base=0, pattern=[[-1, 128]],
         channel_multiplier=1)
    p.op("gpsimd", "memset", tri_f[:], 1.0)
    p.op("gpsimd", "affine_select", out=tri_f[:], in_=tri_f[:],
         compare_op=OP.is_ge, fill=0.0, base=0, pattern=[[1, 64]],
         channel_multiplier=-1, inc=gp)
    gp_consts = gp.v
    p.dma("sync", dbias[:], dtb_in, qs)
    p.dma("sync", alog[:], alog_in, qs)
    p.dma("sync", bnw[:], bnw_in, qs)
    p.dma("sync", cnw[:], cnw_in, qs)
    p.dma("sync", Dbc[:], d_in.partition_broadcast(128), qs)
    qs_consts = qs.v

    A.mark()
    NT = S // 128
    lnwB = A.alloc([128, D], F32)
    lnbB = A.alloc([128, D], F32)
    u_t = [A.alloc([128, D], F32) for _ in range(2)]
    p.dma("sync", lnwB[:], lnw_in.partition_broadcast(128), qs)
    p.dma("sync", lnbB[:], lnb_in.partition_broadcast(128), qs)
    sq_t = A.alloc([128, D], F32)
    st1 = A.alloc([128, 8], F32)
    sd_t = A.alloc([128, 2], F32)
    xn_t = A.alloc([128, D], F32)
    base_t = [A.alloc([128, D], F32) for _ in range(2)]
    un_t = A.alloc([128, D], F32)
    unb_t = A.alloc([128, D], BF16)
    xf_t = A.alloc([128, D], F32)

    inv_d = 1.0 / D

    def ln_stats(src, r_out):
        p.wait("scalar", vp, vp.v)
        p.op("scalar", "activation", sq_t[:], src, AF.Square, inc=sp_)
        sv = sp_.v
        p.op("vector", "reduce_sum", st1[:, 0:1], src, axis=AX.X)
        p.wait("vector", sp_, sv)
        p.op("vector", "reduce_sum", st1[:, 1:2], sq_t[:], axis=AX.X)
        p.op("vector", "tensor_scalar_mul", st1[:, 2:3], st1[:, 0:1], inv_d)
        p.op("vector", "tensor_scalar_mul", st1[:, 3:4], st1[:, 1:2], inv_d)
        p.op("vector", "tensor_mul", st1[:, 4:5], st1[:, 2:3], st1[:, 2:3])
        p.op("vector", "tensor_sub", st1[:, 5:6], st1[:, 3:4], st1[:, 4:5])
        p.op("vector", "tensor_scalar_add", st1[:, 6:7], st1[:, 5:6], EPS)
        p.op("vector", "reciprocal", st1[:, 7:8], st1[:, 6:7], inc=vp)
        vv = vp.v
        p.wait("scalar", vp, vv)
        p.op("scalar", "activation", r_out, st1[:, 7:8], AF.Sqrt, inc=sp_)

    for i in range(NT):
        ub = u_t[i % 2]
        p.dma("sync", ub[:], u_in[i * 128:(i + 1) * 128, :], qs)
        p.wait("vector", qs, qs.v)
        p.wait("vector", qg, qg.v)
        p.wait("scalar", qs, qs.v)
        ln_stats(ub[:], sd_t[:, 0:1])
        p.wait("vector", sp_, sp_.v)
        p.op("vector", "tensor_scalar", xn_t[:], ub[:], st1[:, 2:3],
             sd_t[:, 0:1], OP.subtract, OP.mult, inc=vp)
        p.wait("vector", qg, qg.v)
        p.op("vector", "tensor_add", base_t[i % 2][:], xn_t[:], ub[:], inc=vp)
        p.wait("sync", vp, vp.v)
        p.wait("gpsimd", vp, vp.v)
        p.dma("sync", xn_tm[i * 128:(i + 1) * 128, :], xn_t[:], qs)
        p.dma("gpsimd", out[i * 128:(i + 1) * 128, :], base_t[i % 2][:], qg)
        ln_stats(xn_t[:], sd_t[:, 1:2])
        p.wait("vector", sp_, sp_.v)
        p.op("vector", "tensor_scalar", un_t[:], xn_t[:], st1[:, 2:3],
             sd_t[:, 1:2], OP.subtract, OP.mult)
        p.op("vector", "tensor_mul", un_t[:], un_t[:], lnwB[:])
        p.op("vector", "tensor_add", unb_t[:], un_t[:], lnbB[:], inc=vp)
        p.wait("gpsimd", vp, vp.v)
        p.dma("gpsimd", un_tm[i * 128:(i + 1) * 128, :], unb_t[:], qg)
        p.wait("tensor", vp, vp.v)
        copy_done = []
        for j in range(16):
            bank = ps[:, (j % 2) * 512:(j % 2) * 512 + 128]
            if j >= 2:
                p.wait("tensor", vp, copy_done[j - 2])
            p.op("tensor", "transpose", bank, xn_t[:, j * 128:(j + 1) * 128],
                 ident_f[:], inc=tp)
            p.wait("vector", tp, tp.v)
            p.op("vector", "tensor_copy", xf_t[:, j * 128:(j + 1) * 128], bank,
                 inc=vp)
            copy_done.append(vp.v)
        p.wait("sync", vp, vp.v)
        p.dma("sync", xn_fm32.rearrange("(c q) t -> q c t", q=128)
              [:, :, i * 128:(i + 1) * 128], xf_t[:].rearrange(
                  "q (c t) -> q c t", c=16), qs)
    p.barrier()
    A.release()
    A.mark()
    unf_st = [A.alloc([128, S], BF16) for _ in range(2)]
    store_done = {}
    for j in range(16):
        t = unf_st[j % 2]
        if j >= 2:
            p.wait("sync", qs, store_done[j - 2])
        p.dmat("sync", t[:], un_tm[:, j * 128:(j + 1) * 128], qs)
        p.wait("sync", qs, qs.v)
        p.dma("sync", un_fm[j * 128:(j + 1) * 128, :], t[:], qs)
        store_done[j] = qs.v
    p.barrier()
    A.release()

    if stop == "ph0":
        blk = stack.enter_context(nc.Block())
        p.finalize(blk)
        stack.close()
        return nc

    A.mark()
    Bn = A.alloc([DS, S], BF16)
    Cn = A.alloc([DS, S], BF16)
    einv_cm = A.alloc([64, NCH, NH], F32)
    ec_cm = A.alloc([64, NCH, NH], F32)
    ECSb = A.alloc([128, NCH, NH], F32)
    A.mark()
    wbc = A.alloc([128, 16, 320], BF16)
    unf = A.alloc([128, 16, 2048], BF16)
    Bf = A.alloc([128, S], F32)
    Cf = A.alloc([128, S], F32)
    dtf = A.alloc([64, S], F32)
    sqB = A.alloc([128, S], F32)
    rowA = A.alloc([1, S], F32)
    rBb = A.alloc([128, S], F32)
    ones_f = A.alloc([128, 1], F32)
    Mb = rBb[0:64, :]
    daf = Bf[0:64, :]
    srow = dtf[0:1, :]
    erow = Cf[0:1, :]
    dacm = Cf[0:64, :]
    cumacm = sqB[0:64, :]

    p.op("gpsimd", "memset", ones_f[:], 1.0, inc=gp)
    for dc in range(16):
        p.dma("gpsimd", wbc[:, dc, :],
              in_projT[dc * 128:(dc + 1) * 128, 8192:8512], qg)
        p.dma("sync", unf[:, dc, :], un_fm[dc * 128:(dc + 1) * 128, :], qs)
    p.wait("tensor", qg, qg.v)
    p.wait("tensor", qs, qs.v)
    p.wait("tensor", gp, gp.v)

    def proj_part(e0, e1, dst, banksel):
        m = e1 - e0
        evs = []
        for tt in range(4):
            bank = ps[0:m, (banksel * 4 + tt) * 512:(banksel * 4 + tt) * 512 + 512]
            for dc in range(16):
                p.op("tensor", "matmul", bank, wbc[:, dc, e0:e1],
                     unf[:, dc, tt * 512:(tt + 1) * 512],
                     start=(dc == 0), stop=(dc == 15),
                     inc=(tp if dc == 15 else None))
            p.wait("vector", tp, tp.v)
            p.op("vector", "tensor_copy", dst[:, tt * 512:(tt + 1) * 512], bank,
                 inc=vp)
            evs.append(vp.v)
        return evs

    proj_part(0, 128, Bf, 0)
    proj_part(128, 256, Cf, 1)
    p.wait("tensor", vp, vp.v)
    proj_part(256, 320, dtf, 0)

    def rms_norm(src, w_ap, dst_bf, rtrow):
        p.wait("scalar", vp, vp.v)
        p.op("scalar", "activation", sqB[:], src[:], AF.Square, inc=sp_)
        p.wait("tensor", sp_, sp_.v)
        p.wait("tensor", vp, vp.v)
        for tt in range(4):
            bank = ps[0:1, (4 + tt) * 512:(4 + tt) * 512 + 512]
            p.op("tensor", "matmul", bank, ones_f[:],
                 sqB[:, tt * 512:(tt + 1) * 512], start=True, stop=True,
                 inc=tp)
            p.wait("vector", tp, tp.v)
            p.op("vector", "tensor_copy", rowA[:, tt * 512:(tt + 1) * 512],
                 bank, inc=vp)
        p.op("vector", "tensor_scalar", rowA[:], rowA[:], 1.0 / DS, EPS,
             OP.mult, OP.add, inc=vp)
        p.wait("scalar", vp, vp.v)
        p.op("scalar", "activation", rowA[:], rowA[:], AF.Sqrt, inc=sp_)
        p.wait("vector", sp_, sp_.v)
        p.op("vector", "reciprocal", rowA[:], rowA[:], inc=vp)
        p.wait("sync", vp, vp.v)
        p.dma("sync", rt[rtrow:rtrow + 1, :], rowA[:], qs)
        p.wait("sync", qs, qs.v)
        p.dma("sync", rBb[:], rt[rtrow:rtrow + 1, :].partition_broadcast(128),
              qs)
        p.wait("vector", qs, qs.v)
        p.op("vector", "tensor_mul", sqB[:], src[:], rBb[:])
        p.op("vector", "tensor_scalar", dst_bf[:], sqB[:], w_ap, None,
             OP.mult, inc=vp)

    rms_norm(Bf, bnw[:], Bn, 0)
    rms_norm(Cf, cnw[:], Cn, 1)

    p.op("scalar", "activation", negA[:], alog[:], AF.Exp, inc=sp_)
    p.wait("vector", sp_, sp_.v)
    p.op("vector", "tensor_scalar_mul", negA[:], negA[:], -1.0, inc=vp)
    p.wait("scalar", vp, vp.v)
    p.op("scalar", "activation", daf[:], dtf[:], AF.Exp, bias=dbias[:])
    p.op("scalar", "activation", daf[:], daf[:], AF.Ln, bias=1.0, inc=sp_)
    p.wait("vector", sp_, sp_.v)
    p.op("vector", "tensor_scalar", daf[:], daf[:], negA[:], None, OP.mult,
         inc=vp)
    p.wait("tensor", vp, vp.v)
    cdone = []
    for c in range(NCH):
        bank = ps[0:64, (c % 2) * 512:(c % 2) * 512 + 64]
        if c >= 2:
            p.wait("tensor", vp, cdone[c - 2])
        p.op("tensor", "transpose", bank, daf[:, c * 64:(c + 1) * 64],
             ident_f[0:64, 0:64], inc=tp)
        p.wait("vector", tp, tp.v)
        p.op("vector", "tensor_copy", dacm[:, c * 64:(c + 1) * 64], bank,
             inc=vp)
        cdone.append(vp.v)
    p.wait("tensor", vp, vp.v)
    for tt in range(4):
        bank = ps[0:64, (2 + tt) * 512:(2 + tt) * 512 + 512]
        p.op("tensor", "matmul", bank, tri_f[:],
             dacm[:, tt * 512:(tt + 1) * 512], start=True, stop=True, inc=tp)
        p.wait("vector", tp, tp.v)
        p.op("vector", "tensor_copy", cumacm[:, tt * 512:(tt + 1) * 512],
             bank, inc=vp)
    p.wait("sync", vp, vp.v)
    p.dma("sync", rt[3:4, :], cumacm[0:1, :], qs)
    p.dma("sync", rt[4:5, :], cumacm[63:64, :], qs)
    p.wait("sync", qs, qs.v)
    p.dma("sync", rowA[:], rt[3:4, :], qs)
    p.dma("sync", srow[:], rt[4:5, :], qs)
    p.wait("vector", qs, qs.v)
    p.op("vector", "tensor_add", rowA[:], rowA[:], srow[:])
    p.op("vector", "tensor_scalar_mul", rowA[:], rowA[:], 0.5)
    p.op("vector", "tensor_sub", erow[:], srow[:], rowA[:])
    e3 = erow[:].rearrange("p (c h) -> p c h", c=NCH)
    m3 = rowA[:].rearrange("p (c h) -> p c h", c=NCH)
    p.op("vector", "tensor_add", e3[:, 0:NCH - 1, :], e3[:, 0:NCH - 1, :],
         m3[:, 1:NCH, :], inc=vp)
    p.wait("scalar", vp, vp.v)
    p.op("scalar", "activation", erow[:], erow[:], AF.Exp, inc=sp_)
    p.wait("sync", sp_, sp_.v)
    p.dma("sync", rt[2:3, :], erow[:], qs)
    p.dma("sync", rt[5:6, :], rowA[:], qs)
    p.wait("sync", qs, qs.v)
    p.dma("sync", ECSb.rearrange("p a b -> p (a b)"),
          rt[2:3, :].partition_broadcast(128), qs)
    p.dma("sync", Mb[:], rt[5:6, :].partition_broadcast(64), qs)
    p.wait("vector", qs, qs.v)
    p.op("vector", "tensor_sub", daf[:], Mb[:], cumacm[:])
    p.op("vector", "tensor_scalar_min", daf[:], daf[:], 85.0, inc=vp)
    p.wait("scalar", vp, vp.v)
    p.op("scalar", "activation", einv_cm.rearrange("p a b -> p (a b)"),
         daf[:], AF.Exp, inc=sp_)
    p.wait("vector", sp_, sp_.v)
    p.op("vector", "tensor_sub", daf[:], cumacm[:], Mb[:])
    p.op("vector", "tensor_scalar_min", daf[:], daf[:], 85.0, inc=vp)
    p.wait("scalar", vp, vp.v)
    p.op("scalar", "activation", ec_cm.rearrange("p a b -> p (a b)"),
         daf[:], AF.Exp, inc=sp_)
    p.wait("vector", sp_, sp_.v)
    p.wait("sync", sp_, sp_.v)

    if "bn_tap" in taps:
        bn_tap = dram("bn_tap", [DS, S], BF16)
        cn_tap = dram("cn_tap", [DS, S], BF16)
        cuma_tap = dram("cuma_tap", [64, S], F32)
        p.wait("sync", vp, vp.v)
        p.dma("sync", bn_tap, Bn[:], qs)
        p.dma("sync", cn_tap, Cn[:], qs)
        p.dma("sync", cuma_tap, cumacm[:], qs)

    p.barrier()
    A.release()

    if stop == "ph1":
        blk = stack.enter_context(nc.Block())
        p.finalize(blk)
        stack.close()
        return nc

    for q in range(4):
        A.mark()
        wx = A.alloc([128, 16, 1024], BF16)
        wz = A.alloc([128, 16, 1024], BF16)
        unp = [A.alloc([128, 16, 128], BF16) for _ in range(2)]
        xpair = [A.alloc([128, 1024], BF16) for _ in range(2)]
        szpair = [A.alloc([128, 1024], BF16) for _ in range(2)]
        xsh = A.alloc([64, 1024], BF16)
        szsh = A.alloc([64, 1024], BF16)
        xt = A.alloc([64, 1024], BF16)
        btb = A.alloc([64, 128], BF16)
        mmsk = A.alloc([64, 64], BF16)
        stA = A.alloc([128, 1024], BF16)
        stB = A.alloc([128, 1024], BF16)
        sttmp = A.alloc([128, 1024], BF16)
        t1 = A.alloc([64, 1024], F32)
        t2 = A.alloc([64, 1024], F32)
        ygb = A.alloc([64, 1024], BF16)

        EX = 4096 + 1024 * q
        EZ = 1024 * q
        H0 = 16 * q

        for dc in range(16):
            p.dma("gpsimd", wx[:, dc, :],
                  in_projT[dc * 128:(dc + 1) * 128, EX:EX + 1024], qg)
        wx_ready = qg.v
        for dc in range(16):
            p.dma("gpsimd", wz[:, dc, :],
                  in_projT[dc * 128:(dc + 1) * 128, EZ:EZ + 1024], qg)
        wz_ready = qg.v
        p.op("vector", "memset", stA[:], 0.0, inc=vp)

        ps_sc = ps[0:64, 3072:3136]
        ps_btv = ps_bf[0:64, 6400:6528]
        xz_free = sp_.v
        y_free = vp.v
        p_free = vp.v
        sc_free = vp.v
        st_upd = vp.v
        un_rdy = {}
        z_done = {}
        yg_dma = 0
        for pr in range(16):
            up = unp[pr % 2]
            if pr >= 2:
                p.wait("sync", tp, z_done[pr - 2])
            un_rdy[pr] = p.dma(
                "sync", up[:],
                un_fm.rearrange("(c pp) t -> pp c t", pp=128)
                [:, :, pr * 128:(pr + 1) * 128], qs)
            p.wait("tensor", qs, un_rdy[pr])
            if pr == 0:
                p.wait("tensor", qg, wx_ready)
            p.wait("tensor", sp_, xz_free)
            for et in range(2):
                bank = ps[:, et * 512:et * 512 + 512]
                for dc in range(16):
                    p.op("tensor", "matmul", bank, up[:, dc, :],
                         wx[:, dc, et * 512:et * 512 + 512],
                         start=(dc == 0), stop=(dc == 15),
                         inc=(tp if dc == 15 else None))
            p.wait("vector", tp, tp.v)
            p.wait("vector", qg, qg.v)
            p.op("vector", "tensor_copy", xpair[pr % 2][:], ps[:, 0:1024],
                 inc=vp)
            x_ev = vp.v
            p.wait("tensor", vp, x_ev)
            if pr == 0:
                p.wait("tensor", qg, wz_ready)
            for et in range(2):
                bank = ps[:, et * 512:et * 512 + 512]
                for dc in range(16):
                    p.op("tensor", "matmul", bank, up[:, dc, :],
                         wz[:, dc, et * 512:et * 512 + 512],
                         start=(dc == 0), stop=(dc == 15),
                         inc=(tp if dc == 15 else None))
            z_done[pr] = tp.v
            p.wait("scalar", tp, tp.v)
            p.wait("scalar", qg, qg.v)
            p.op("scalar", "activation", szpair[pr % 2][:], ps[:, 0:1024],
                 AF.Silu, inc=sp_)
            sz_ev = xz_free = sp_.v

            for ci in range(2):
                c = 2 * pr + ci
                if ci == 0:
                    xc, szc = xpair[pr % 2][0:64, :], szpair[pr % 2][0:64, :]
                    xc_rdy, sz_rdy = x_ev, sz_ev
                else:
                    p.wait("gpsimd", vp, x_ev)
                    p.wait("gpsimd", sp_, sz_ev)
                    p.dma("gpsimd", xsh[:], xpair[pr % 2][64:128, :], qg)
                    p.dma("gpsimd", szsh[:], szpair[pr % 2][64:128, :], qg)
                    xc, szc = xsh, szsh
                    xc_rdy, sz_rdy = None, None
                    sh_rdy = qg.v
                p.wait("tensor", vp, sc_free)
                p.op("tensor", "matmul", ps_sc, Bn[:, c * 64:(c + 1) * 64],
                     Cn[:, c * 64:(c + 1) * 64], start=True, stop=True)
                p.op("tensor", "transpose", ps_btv, Bn[:, c * 64:(c + 1) * 64],
                     ident_bf[:], inc=tp)
                p.wait("vector", tp, tp.v)
                if ci == 1:
                    p.wait("vector", qg, sh_rdy)
                p.op("vector", "tensor_mul", mmsk[:], ps_sc, tri_f[:])
                p.op("vector", "tensor_copy", btb[:], ps_btv)
                p.op("vector", "tensor_tensor",
                     xt[:].rearrange("p (h j) -> p h j", h=16),
                     xc.rearrange("p (h j) -> p h j", h=16),
                     einv_cm[:, c, H0:H0 + 16].unsqueeze(2)
                     .broadcast_to([64, 16, 64]), OP.mult, inc=vp)
                sc_free = pre_y = vp.v
                p.wait("tensor", vp, pre_y)
                stcur = stA if c % 2 == 0 else stB
                stnew = stB if c % 2 == 0 else stA
                for nt in range(2):
                    ybank = ps[0:64, 1024 + nt * 512:1024 + nt * 512 + 512]
                    p.op("tensor", "matmul", ybank, mmsk[:],
                         xt[:, nt * 512:nt * 512 + 512],
                         start=True, stop=False)
                    p.op("tensor", "matmul", ybank,
                         Cn[:, c * 64:(c + 1) * 64],
                         stcur[:, nt * 512:nt * 512 + 512],
                         start=False, stop=True,
                         inc=(tp if nt == 1 else None))
                y_mm = tp.v
                for nt in range(2):
                    pbank = ps[:, 2048 + nt * 512:2048 + nt * 512 + 512]
                    p.op("tensor", "matmul", pbank, btb[:],
                         xt[:, nt * 512:nt * 512 + 512],
                         start=True, stop=True,
                         inc=(tp if nt == 1 else None))
                p_mm = tp.v
                p.wait("vector", tp, y_mm)
                p.op("vector", "tensor_tensor",
                     t1[:].rearrange("p (h j) -> p h j", h=16),
                     ps[0:64, 1024:2048].rearrange("p (h j) -> p h j", h=16),
                     ec_cm[:, c, H0:H0 + 16].unsqueeze(2)
                     .broadcast_to([64, 16, 64]), OP.mult)
                p.op("vector", "tensor_tensor",
                     t2[:].rearrange("p (h j) -> p h j", h=16),
                     xc.rearrange("p (h j) -> p h j", h=16),
                     Dbc[0:64, H0:H0 + 16].unsqueeze(2)
                     .broadcast_to([64, 16, 64]), OP.mult)
                p.op("vector", "tensor_add", t1[:], t1[:], t2[:])
                if ci == 0:
                    p.wait("vector", sp_, sz_ev)
                p.wait("vector", qs, yg_dma)
                p.op("vector", "tensor_mul", ygb[:], t1[:], szc, inc=vp)
                y_free = vp.v
                p.wait("sync", vp, y_free)
                yg_dma = p.dma("sync", yg_tm[c * 64:(c + 1) * 64,
                                             1024 * q:1024 * (q + 1)],
                               ygb[:], qs)
                p.wait("vector", tp, p_mm)
                p.op("vector", "tensor_tensor", sttmp[:], ps[:, 2048:3072],
                     stcur[:], OP.add)
                p.op("vector", "tensor_tensor",
                     stnew[:].rearrange("p (h j) -> p h j", h=16),
                     sttmp[:].rearrange("p (h j) -> p h j", h=16),
                     ECSb[:, c, H0:H0 + 16].unsqueeze(2)
                     .broadcast_to([128, 16, 64]), OP.mult, inc=vp)
                st_upd = vp.v
                p_free = vp.v
        p.barrier()
        A.release()
    A.release()

    if stop == "ph2":
        blk = stack.enter_context(nc.Block())
        p.finalize(blk)
        stack.close()
        return nc

    A.mark()
    ygf3 = A.alloc([128, 16, 2048], BF16)
    wob3 = A.alloc([128, 16, 2048], BF16)
    stage = [A.alloc([128, 2048], F32) for _ in range(2)]
    ev_done = {}
    mm3_done = 0
    out_w = 0
    for hf in range(2):
        ygf_rdy = wob_rdy = 0
        for ec in range(16):
            gc = hf * 16 + ec
            if hf > 0:
                p.wait("sync", tp, mm3_done)
                p.wait("gpsimd", tp, mm3_done)
            ygf_rdy = p.dmat("sync", ygf3[:, ec, :],
                             yg_tm[:, gc * 128:(gc + 1) * 128], qs)
            wob_rdy = p.dma("gpsimd", wob3[:, ec, :],
                            woutT[gc * 128:(gc + 1) * 128, :], qg)
        p.wait("tensor", qs, ygf_rdy)
        p.wait("tensor", qg, wob_rdy)
        if hf > 0:
            p.wait("gpsimd", qg, out_w)
        for tc in range(16):
            for dt_ in range(4):
                bank = ps[:, (dt_ % 2) * 512:(dt_ % 2) * 512 + 512]
                prev = (tc, dt_ - 2) if dt_ >= 2 else (tc - 1, dt_ + 2)
                if prev in ev_done:
                    p.wait("tensor", vp, ev_done[prev])
                for ec in range(16):
                    p.op("tensor", "matmul", bank,
                         ygf3[:, ec, tc * 128:(tc + 1) * 128],
                         wob3[:, ec, dt_ * 512:(dt_ + 1) * 512],
                         start=(ec == 0), stop=(ec == 15),
                         inc=(tp if ec == 15 else None))
                p.wait("vector", tp, tp.v)
                if dt_ == 0 and (tc >= 2 or hf > 0):
                    p.wait("vector", qg, ev_done.get(("dma", tc - 2, hf), 0)
                           if tc >= 2 else out_w)
                sg = stage[tc % 2][:, dt_ * 512:(dt_ + 1) * 512]
                p.op("vector", "tensor_copy", sg, bank, inc=vp)
                ev_done[(tc, dt_)] = vp.v
            p.wait("gpsimd", vp, vp.v)
            ev_done[("dma", tc, hf)] = out_w = p.dma(
                "gpsimd", out[tc * 128:(tc + 1) * 128, :], stage[tc % 2][:],
                qg, accum_op=OP.add)
        mm3_done = tp.v
    p.barrier()
    A.release()

    if stop == "ph3":
        blk = stack.enter_context(nc.Block())
        p.finalize(blk)
        stack.close()
        return nc

    A.mark()
    xcol = [A.alloc([128, 16, 128], F32) for _ in range(2)]
    gate_sb = A.alloc([128, 16, 8], F32)
    sgt_sb = A.alloc([128, 16], F32)
    lg = A.alloc([128, 16, 8], F32)
    ro = A.alloc([128, 16, 8], F32)
    rank = A.alloc([128, 16, 8], F32)
    tmp4 = A.alloc([128, 16, 8], F32)
    msk4 = A.alloc([128, 16, 8], F32)
    mx4 = A.alloc([128, 16, 1], F32)
    io8 = A.alloc([128, 8], F32)
    io8i = A.alloc([128, 8], mybir.dt.int32)
    tb = A.alloc([128, 7, 8], F32)
    sgrow = A.alloc([1, S], F32)
    combf = A.alloc([8, S], F32)

    p.dma("sync", gate_sb[:],
          gateT.rearrange("(c p) e -> p c e", p=128), qs)
    p.dma("sync", sgt_sb[:], sgT, qs)
    p.op("gpsimd", "iota", io8i[:], pattern=[[1, 8]], base=0,
         channel_multiplier=0, inc=gp)
    p.wait("vector", gp, gp.v)
    p.op("vector", "tensor_copy", io8[:], io8i[:])
    for s_ in range(1, 8):
        p.op("vector", "tensor_single_scalar", tb[:, s_ - 1, :], io8[:],
             float(s_), OP.is_ge)
    ld_prev = 0
    for tc in range(16):
        xc4 = xcol[tc % 2]
        if tc >= 2:
            p.wait("sync", tp, ld_prev)
        p.dma("sync", xc4[:],
              xn_fm32.rearrange("(c p) t -> p c t", p=128)
              [:, :, tc * 128:(tc + 1) * 128], qs)
        p.wait("tensor", qs, qs.v)
        bl = ps[:, 0:8]
        bs = ps[0:1, 512:640]
        for dc in range(16):
            p.op("tensor", "matmul", bl, xc4[:, dc, :], gate_sb[:, dc, :],
                 start=(dc == 0), stop=(dc == 15),
                 inc=(tp if dc == 15 else None))
        for dc in range(16):
            p.op("tensor", "matmul", bs, sgt_sb[:, dc:dc + 1], xc4[:, dc, :],
                 start=(dc == 0), stop=(dc == 15),
                 inc=(tp if dc == 15 else None))
        ld_prev = tp.v
        p.wait("vector", tp, tp.v)
        p.op("vector", "tensor_copy", lg[:, tc, :], bl)
        p.op("vector", "tensor_copy", sgrow[:, tc * 128:(tc + 1) * 128], bs,
             inc=vp)
        p.wait("tensor", vp, vp.v)
    p.op("vector", "memset", rank[:], 0.0)
    for s_ in range(1, 8):
        p.op("vector", "tensor_copy", ro[:, :, s_:8], lg[:, :, 0:8 - s_])
        p.op("vector", "tensor_copy", ro[:, :, 0:s_], lg[:, :, 8 - s_:8])
        p.op("vector", "tensor_tensor", tmp4[:], ro[:], lg[:], OP.is_gt)
        p.op("vector", "tensor_add", rank[:], rank[:], tmp4[:])
        p.op("vector", "tensor_tensor", tmp4[:], ro[:], lg[:], OP.is_equal)
        p.op("vector", "tensor_mul", tmp4[:], tmp4[:],
             tb[:, s_ - 1:s_, :].broadcast_to([128, 16, 8]))
        p.op("vector", "tensor_add", rank[:], rank[:], tmp4[:])
    p.op("vector", "tensor_single_scalar", msk4[:], rank[:], 3.0, OP.is_le)
    p.op("vector", "reduce_max", mx4[:], lg[:], axis=AX.X)
    p.op("vector", "tensor_sub", tmp4[:], lg[:],
         mx4[:].broadcast_to([128, 16, 8]), inc=vp)
    p.wait("scalar", vp, vp.v)
    p.op("scalar", "activation", tmp4[:], tmp4[:], AF.Exp, inc=sp_)
    p.wait("vector", sp_, sp_.v)
    p.op("vector", "tensor_mul", tmp4[:], tmp4[:], msk4[:])
    p.op("vector", "reduce_sum", mx4[:], tmp4[:], axis=AX.X)
    p.op("vector", "reciprocal", mx4[:], mx4[:])
    p.op("vector", "tensor_mul", tmp4[:], tmp4[:],
         mx4[:].broadcast_to([128, 16, 8]), inc=vp)
    p.wait("tensor", vp, vp.v)
    cdone = []
    for tc in range(16):
        bank = ps[0:8, (tc % 2) * 512:(tc % 2) * 512 + 128]
        if tc >= 2:
            p.wait("tensor", vp, cdone[tc - 2])
        p.op("tensor", "transpose", bank, tmp4[:, tc, :], ident_f[:], inc=tp)
        p.wait("vector", tp, tp.v)
        p.op("vector", "tensor_copy", combf[:, tc * 128:(tc + 1) * 128], bank,
             inc=vp)
        cdone.append(vp.v)
    p.wait("sync", vp, vp.v)
    p.dma("sync", comb_fm[0:8, :], combf[:], qs)
    p.op("scalar", "activation", sgrow[:], sgrow[:], AF.Sigmoid, inc=sp_)
    p.wait("sync", sp_, sp_.v)
    for g in range(4):
        p.dma("sync", comb_fm[8 + g:9 + g, :], sgrow[:], qs)
    p.barrier()
    A.release()

    if stop == "ph4":
        blk = stack.enter_context(nc.Block())
        p.finalize(blk)
        stack.close()
        return nc

    A.mark()
    xnb = A.alloc([128, 16, 2048], BF16)
    actk = A.alloc([128, 11, 2048], BF16)
    wdk = A.alloc([128, 11, 2048], BF16)
    combB = A.alloc([128, 2048], BF16)
    t5sg = [A.alloc([128, 512], F32) for _ in range(2)]
    for dc in range(16):
        p.dma("gpsimd", xnb[:, dc, :], xn_fm32[dc * 128:(dc + 1) * 128, :], qg)
    xnb_rdy = qg.v
    wgc = [A.alloc([128, 16, 128], BF16) for _ in range(2)]
    wuc = [A.alloc([128, 16, 128], BF16) for _ in range(2)]
    dsA = [A.alloc([128, 1024], F32) for _ in range(2)]
    dsB = [A.alloc([128, 1024], F32) for _ in range(2)]

    def exp_srcs(k, ic):
        if k < 8:
            return (wgT[k][:, ic * 128:(ic + 1) * 128],
                    wuT[k][:, ic * 128:(ic + 1) * 128],
                    wdT[k][ic * 128:(ic + 1) * 128, :])
        g = k - 8
        o = g * I_EXP + ic * 128
        return (swgT[:, o:o + 128], swuT[:, o:o + 128], swdT[o:o + 128, :])

    wg_s, wu_s, _ = exp_srcs(0, 0)
    p.dma("gpsimd", wgc[0][:], wg_s.rearrange("(c p) i -> p c i", p=128), qg)
    p.dma("gpsimd", wuc[0][:], wu_s.rearrange("(c p) i -> p c i", p=128), qg)
    w_rdy = {(0, 0): qg.v}

    prev_accum = 0
    up_ev_prev = 0
    buf_last_use = {0: 0, 1: 0}
    slice_ev = {}
    up_bank_ev = []
    for k in range(NEXP):
        p.wait("gpsimd", vp, up_ev_prev)
        comb_rdy = p.dma("gpsimd", combB[:],
                         comb_fm[k:k + 1, :].partition_broadcast(128), qg)
        wd_rdy = []
        for ic in range(11):
            _, _, wd_s = exp_srcs(k, ic)
            wd_rdy.append(p.dma("gpsimd", wdk[:, ic, :], wd_s, qg))
        for ic in range(11):
            nk, nic = (k, ic + 1) if ic < 10 else (k + 1, 0)
            if nk < NEXP:
                par = (ic + 1) % 2
                p.wait("gpsimd", tp, buf_last_use[par])
                wg_s, wu_s, _ = exp_srcs(nk, nic)
                p.dma("gpsimd", wgc[par][:],
                      wg_s.rearrange("(c p) i -> p c i", p=128), qg)
                p.dma("gpsimd", wuc[par][:],
                      wu_s.rearrange("(c p) i -> p c i", p=128), qg)
                w_rdy[(nk, nic)] = qg.v
            p.wait("tensor", qg, w_rdy[(k, ic)])
            if k == 0 and ic == 0:
                p.wait("tensor", qg, xnb_rdy)
            for tt in range(4):
                bg = ps[:, (tt % 2) * 1024:(tt % 2) * 1024 + 512]
                bu = ps[:, (tt % 2) * 1024 + 512:(tt % 2) * 1024 + 1024]
                if len(up_bank_ev) >= 2:
                    p.wait("tensor", vp, up_bank_ev[-2])
                for dc in range(16):
                    p.op("tensor", "matmul", bg, wgc[ic % 2][:, dc, :],
                         xnb[:, dc, tt * 512:(tt + 1) * 512],
                         start=(dc == 0), stop=(dc == 15))
                for dc in range(16):
                    p.op("tensor", "matmul", bu, wuc[ic % 2][:, dc, :],
                         xnb[:, dc, tt * 512:(tt + 1) * 512],
                         start=(dc == 0), stop=(dc == 15),
                         inc=(tp if dc == 15 else None))
                buf_last_use[ic % 2] = tp.v
                p.wait("scalar", tp, tp.v)
                p.wait("scalar", vp, slice_ev.get(tt % 2, 0))
                sgb = t5sg[tt % 2]
                p.op("scalar", "activation", sgb[:], bg, AF.Silu, inc=sp_)
                p.wait("vector", sp_, sp_.v)
                if ic == 0 and tt == 0:
                    p.wait("vector", qg, comb_rdy)
                sl = slice(tt * 512, (tt + 1) * 512)
                p.op("vector", "tensor_mul", sgb[:], sgb[:], bu)
                p.op("vector", "tensor_mul", actk[:, ic, sl], sgb[:],
                     combB[:, sl], inc=vp)
                slice_ev[tt % 2] = vp.v
                up_bank_ev.append(vp.v)
        up_ev_prev = vp.v
        p.wait("tensor", qg, wd_rdy[-1])
        p.wait("tensor", vp, up_ev_prev)
        if k > 0:
            p.wait("gpsimd", qg, prev_accum)
        ds_dma = {}
        dn_ev = {}
        for tc in range(16):
            for dt_ in range(4):
                bank = ps[:, 2048 + (dt_ % 2) * 512:2048 + (dt_ % 2) * 512 + 512]
                prev = (tc, dt_ - 2) if dt_ >= 2 else (tc - 1, dt_ + 2)
                if prev in dn_ev:
                    p.wait("tensor", vp, dn_ev[prev])
                for ic in range(11):
                    p.op("tensor", "matmul", bank,
                         actk[:, ic, tc * 128:(tc + 1) * 128],
                         wdk[:, ic, dt_ * 512:(dt_ + 1) * 512],
                         start=(ic == 0), stop=(ic == 10),
                         inc=(tp if ic == 10 else None))
                p.wait("vector", tp, tp.v)
                if dt_ == 0 and tc >= 2:
                    p.wait("vector", qg, ds_dma[tc - 2])
                half = dsA[tc % 2] if dt_ < 2 else dsB[tc % 2]
                p.op("vector", "tensor_copy",
                     half[:, (dt_ % 2) * 512:(dt_ % 2) * 512 + 512], bank,
                     inc=vp)
                dn_ev[(tc, dt_)] = vp.v
            p.wait("gpsimd", vp, vp.v)
            p.dma("gpsimd", out[tc * 128:(tc + 1) * 128, 0:1024],
                  dsA[tc % 2][:], qg, accum_op=OP.add)
            ds_dma[tc] = p.dma("gpsimd",
                               out[tc * 128:(tc + 1) * 128, 1024:2048],
                               dsB[tc % 2][:], qg, accum_op=OP.add)
        prev_accum = qg.v
    p.barrier()
    A.release()

    blk = stack.enter_context(nc.Block())
    p.finalize(blk)
    stack.close()
    return nc



def kernel(**inputs):
    w = inputs
    c = np.ascontiguousarray
    in_projT = c(w["L0_attn_in_proj_w"].T)
    woutT = c(w["L0_attn_out_proj_w"].T)
    gateT = c(w["L0_ffn_gate_w"].T)
    sgT = c(w["L0_ffn_shared_gate_w"].reshape(16, 128).T)
    wgT = c(w["L0_ffn_wg"].transpose(0, 2, 1))
    wuT = c(w["L0_ffn_wu"].transpose(0, 2, 1))
    wdT = c(w["L0_ffn_wd"].transpose(0, 2, 1))
    swgT = c(w["L0_ffn_swg"].T)
    swuT = c(w["L0_ffn_swu"].T)
    swdT = c(w["L0_ffn_swd"].T)
    base = {
        "in_projT": in_projT, "woutT": woutT, "gateT": gateT, "sgT": sgT,
        "wgT": wgT, "wuT": wuT, "wdT": wdT, "swgT": swgT, "swuT": swuT,
        "swdT": swdT,
        "Dh": c(w["L0_attn_D"].reshape(1, NH)),
        "dt_bias": c(w["L0_attn_dt_bias"].reshape(NH, 1)),
        "A_log": c(w["L0_attn_A_log"].reshape(NH, 1)),
        "bnorm_w": c(w["L0_attn_bnorm_w"].reshape(DS, 1)),
        "cnorm_w": c(w["L0_attn_cnorm_w"].reshape(DS, 1)),
        "ln_w": c(w["L0_attn_ln_w"].reshape(1, D)),
        "ln_b": c(w["L0_attn_ln_b"].reshape(1, D)),
    }
    u = w["L0_attn_u"]
    in_maps = []
    for b in range(8):
        m = dict(base)
        m["u"] = c(u[b])
        in_maps.append(m)
    nc = build_program(taps=DEV_TAPS, stop=DEV_STOP)
    res = run_bass_kernel_spmd(nc, in_maps, core_ids=list(range(8)))
    if DEV_TAPS:
        kernel.dev_results = res.results
    return np.stack([r["out"] for r in res.results], axis=0)


# version 09
# speedup vs baseline: 1.3082x; 1.1371x over previous
import math
import numpy as np
import ml_dtypes
import concourse.bass as bass
import concourse.mybir as mybir
from concourse.bass_utils import run_bass_kernel_spmd

F32 = mybir.dt.float32
BF16 = mybir.dt.bfloat16
OP = mybir.AluOpType
AF = mybir.ActivationFunctionType
AX = mybir.AxisListType

S = 2048
D = 2048
NH = 64
HD = 64
DS = 128
CH = 64
NCH = S // CH
DIN = 4096
DPROJ = 8512
NEXP = 12
I_EXP = 1408
EPS = 1e-5

DEV_TAPS = ()
DEV_STOP = None

ENGINES = ("sync", "scalar", "vector", "tensor", "gpsimd")


class Sem:
    def __init__(self, h):
        self.h = h
        self.v = 0


class P:

    def __init__(self, nc, stack):
        self.nc = nc
        self.stack = stack
        self.streams = {e: [] for e in ENGINES}
        self.sems = {}
        self.barv = 0
        self._uid = 0
        self.HAZ_WIN = 2
        self.recent = {e: [] for e in ENGINES}
        self.junk = None

    @staticmethod
    def _apkey(ap):
        try:
            esz = {F32: 4, BF16: 2, mybir.dt.uint8: 1}.get(ap.dtype, 4)
            b0 = ap.offset * esz
            if ap.ndim > 2:
                return (ap.name, b0, b0 + (1 << 40))
            return (ap.name, b0, b0 + ap.free_nbytes())
        except Exception:
            return None

    @staticmethod
    def _overlap(k1, k2):
        return (k1 is not None and k2 is not None and k1[0] == k2[0]
                and k1[1] < k2[2] and k2[1] < k1[2])

    def _spacer(self, eng):
        j = self.junk
        if eng == "vector":
            self.emit(eng, lambda e: e.tensor_copy(j[:, 0:4], j[:, 4:8]))
        elif eng == "scalar":
            self.emit(eng, lambda e: e.copy(j[:, 0:4], j[:, 4:8]))
        elif eng == "gpsimd":
            self.emit(eng, lambda e: e.tensor_copy(j[:, 0:4], j[:, 4:8]))
        else:
            self.emit(eng, lambda e: e.nop())
        self.recent[eng].append(None)
        del self.recent[eng][:-self.HAZ_WIN]

    def sem(self, name):
        s = Sem(self.stack.enter_context(self.nc.semaphore(name)))
        self.sems[name] = s
        return s

    def emit(self, eng, fn):
        self.streams[eng].append(fn)

    def wait(self, eng, sem, val=None):
        v = sem.v if val is None else val
        self.emit(eng, lambda e, h=sem.h, v=v: e.wait_ge(h, v))

    def dma(self, q, out, in_, sem, inc=True, **kw):
        def fn(e, out=out, in_=in_, kw=kw, h=sem.h):
            e.dma_start(out, in_, **kw).then_inc(h, 16)
        self.emit(q, fn)
        sem.v += 16
        return sem.v

    def dmat(self, q, out, in_, sem):
        def fn(e, out=out, in_=in_, h=sem.h):
            e.dma_start(out, in_, transpose=True).then_inc(h, 16)
        self.emit(q, fn)
        sem.v += 16
        return sem.v

    def op(self, eng, method, *args, inc=None, **kw):
        aps = [a for a in args if isinstance(a, bass.AP)]
        kw_out = kw.get("out")
        if kw_out is not None:
            wr = kw_out
            rds = aps + [v for k, v in kw.items()
                         if k != "out" and isinstance(v, bass.AP)]
        else:
            wr = aps[0] if aps else None
            rds = aps[1:] + [v for v in kw.values() if isinstance(v, bass.AP)]
        rkeys = [self._apkey(a) for a in rds]
        guard = 0
        while any(self._overlap(rk, wk) for rk in rkeys
                  for wk in self.recent[eng]):
            self._spacer(eng)
            guard += 1
            assert guard <= self.HAZ_WIN + 1
        if inc is None:
            def fn(e, m=method, a=args, kw=kw):
                getattr(e, m)(*a, **kw)
        else:
            inc.v += 1

            def fn(e, m=method, a=args, kw=kw, h=inc.h):
                getattr(e, m)(*a, **kw).then_inc(h, 1)
        self.emit(eng, fn)
        self.recent[eng].append(self._apkey(wr))
        del self.recent[eng][:-self.HAZ_WIN]
        return inc.v if inc is not None else None

    def barrier(self):
        bar = self.sems["bar"]
        for eng in ENGINES:
            if eng in ("sync", "gpsimd", "scalar"):
                qs = self.sems["q_" + eng]
                self.wait(eng, qs, qs.v)
            self.emit(eng, lambda e, h=bar.h: e.nop().then_inc(h, 1))
        self.barv += 5
        for eng in ENGINES:
            self.wait(eng, bar, self.barv)

    def finalize(self, block):
        for eng in ENGINES:
            stream = self.streams[eng]
            getattr(block, eng)(lambda e, stream=stream: [f(e) for f in stream])


class Arena:

    def __init__(self, nc, stack, bytes_per_part=200 * 1024):
        self.cap = bytes_per_part
        self.t = stack.enter_context(
            nc.sbuf_tensor("arena", [128, bytes_per_part], mybir.dt.uint8))
        self.off = 0
        self.marks = []

    def alloc(self, shape, dt):
        esz = {F32: 4, BF16: 2, mybir.dt.int32: 4}[dt]
        free = int(np.prod(shape[1:]))
        nbytes = free * esz
        off = (self.off + 511) & ~511
        assert off + nbytes <= self.cap, f"SBUF arena overflow: {off + nbytes}"
        self.off = off + nbytes
        v = self.t[0:shape[0], off:off + nbytes].bitcast(dt)
        if len(shape) == 2:
            return v
        pat = "p (" + " ".join(f"d{i}" for i in range(1, len(shape))) + ") -> p " + \
              " ".join(f"d{i}" for i in range(1, len(shape)))
        return v.rearrange(pat, **{f"d{i}": shape[i] for i in range(1, len(shape) - 1)})

    def mark(self):
        self.marks.append(self.off)

    def release(self):
        self.off = self.marks.pop()


def ceil_div(a, b):
    return (a + b - 1) // b



def build_program(taps=(), stop=None):
    from contextlib import ExitStack
    stack = ExitStack()
    nc = bass.Bass("TRN2", target_bir_lowering=False)

    def dram(name, shape, dt, kind="Internal"):
        k = "ExternalOutput" if (name in taps and kind == "Internal") else kind
        return nc.dram_tensor(name, shape, dt, kind=k).ap()

    u_in = dram("u", [S, D], F32, "ExternalInput")
    in_projT = dram("in_projT", [D, DPROJ], F32, "ExternalInput")
    woutT = dram("woutT", [DIN, D], F32, "ExternalInput")
    gateT = dram("gateT", [D, 8], F32, "ExternalInput")
    sgT = dram("sgT", [128, 16], F32, "ExternalInput")
    wgT = dram("wgT", [8, D, I_EXP], F32, "ExternalInput")
    wuT = dram("wuT", [8, D, I_EXP], F32, "ExternalInput")
    wdT = dram("wdT", [8, I_EXP, D], F32, "ExternalInput")
    swgT = dram("swgT", [D, 4 * I_EXP], F32, "ExternalInput")
    swuT = dram("swuT", [D, 4 * I_EXP], F32, "ExternalInput")
    swdT = dram("swdT", [4 * I_EXP, D], F32, "ExternalInput")
    d_in = dram("Dh", [1, NH], F32, "ExternalInput")
    dtb_in = dram("dt_bias", [NH, 1], F32, "ExternalInput")
    alog_in = dram("A_log", [NH, 1], F32, "ExternalInput")
    bnw_in = dram("bnorm_w", [DS, 1], F32, "ExternalInput")
    cnw_in = dram("cnorm_w", [DS, 1], F32, "ExternalInput")
    lnw_in = dram("ln_w", [1, D], F32, "ExternalInput")
    lnb_in = dram("ln_b", [1, D], F32, "ExternalInput")

    out = dram("out", [S, D], F32, "ExternalOutput")

    xn_tm = dram("xn_tm", [S, D], F32)
    xn_fm32 = dram("xn_fm32", [D, S], F32)
    un_tm = dram("un_tm", [S, D], BF16)
    un_fm = dram("un_fm", [D, S], BF16)
    yg_tm = dram("yg_tm", [S, DIN], BF16)
    comb_fm = dram("comb_fm", [NEXP, S], F32)
    rt = dram("rt", [64, S], F32)
    einv_dram = dram("einv_dram", [64, NCH * NH], F32)

    p = P(nc, stack)
    for name in ("q_sync", "q_gpsimd", "q_scalar", "tp", "vp", "sp", "gp", "bar"):
        p.sem(name)
    qs, qg = p.sems["q_sync"], p.sems["q_gpsimd"]
    tp, vp, sp_, gp = p.sems["tp"], p.sems["vp"], p.sems["sp"], p.sems["gp"]

    A = Arena(nc, stack, bytes_per_part=198 * 1024)
    ps = stack.enter_context(nc.psum_tensor("ps", [128, 4096], F32))
    ps_bf = ps.ap().bitcast(BF16)

    p.junk = A.alloc([128, 8], F32)
    ident_bf = A.alloc([128, 128], BF16)
    ident_f = A.alloc([128, 128], F32)
    tri_f = A.alloc([64, 64], F32)
    dbias = A.alloc([NH, 1], F32)
    alog = A.alloc([NH, 1], F32)
    negA = A.alloc([NH, 1], F32)
    bnw = A.alloc([DS, 1], F32)
    cnw = A.alloc([DS, 1], F32)
    Dbc = A.alloc([128, NH], F32)

    p.op("gpsimd", "memset", ident_bf[:], 0.0)
    p.op("gpsimd", "affine_select", out=ident_bf[:], in_=ident_bf[:],
         compare_op=OP.not_equal, fill=1.0, base=0, pattern=[[-1, 128]],
         channel_multiplier=1)
    p.op("gpsimd", "memset", ident_f[:], 0.0)
    p.op("gpsimd", "affine_select", out=ident_f[:], in_=ident_f[:],
         compare_op=OP.not_equal, fill=1.0, base=0, pattern=[[-1, 128]],
         channel_multiplier=1)
    p.op("gpsimd", "memset", tri_f[:], 1.0)
    p.op("gpsimd", "affine_select", out=tri_f[:], in_=tri_f[:],
         compare_op=OP.is_ge, fill=0.0, base=0, pattern=[[1, 64]],
         channel_multiplier=-1, inc=gp)
    gp_consts = gp.v
    p.dma("sync", dbias[:], dtb_in, qs)
    p.dma("sync", alog[:], alog_in, qs)
    p.dma("sync", bnw[:], bnw_in, qs)
    p.dma("sync", cnw[:], cnw_in, qs)
    p.dma("sync", Dbc[:], d_in.partition_broadcast(128), qs)
    qs_consts = qs.v

    A.mark()
    NT = S // 128
    lnwB = A.alloc([128, D], F32)
    lnbB = A.alloc([128, D], F32)
    u_t = [A.alloc([128, D], F32) for _ in range(2)]
    p.dma("sync", lnwB[:], lnw_in.partition_broadcast(128), qs)
    p.dma("sync", lnbB[:], lnb_in.partition_broadcast(128), qs)
    sq_t = A.alloc([128, D], F32)
    st1 = A.alloc([128, 8], F32)
    sd_t = A.alloc([128, 2], F32)
    xn_t = A.alloc([128, D], F32)
    base_t = [A.alloc([128, D], F32) for _ in range(2)]
    un_t = A.alloc([128, D], F32)
    unb_t = A.alloc([128, D], BF16)
    xf_t = A.alloc([128, D], F32)

    inv_d = 1.0 / D

    def ln_stats(src, r_out):
        p.wait("scalar", vp, vp.v)
        p.op("scalar", "activation", sq_t[:], src, AF.Square, inc=sp_)
        sv = sp_.v
        p.op("vector", "reduce_sum", st1[:, 0:1], src, axis=AX.X)
        p.wait("vector", sp_, sv)
        p.op("vector", "reduce_sum", st1[:, 1:2], sq_t[:], axis=AX.X)
        p.op("vector", "tensor_scalar_mul", st1[:, 2:3], st1[:, 0:1], inv_d)
        p.op("vector", "tensor_scalar_mul", st1[:, 3:4], st1[:, 1:2], inv_d)
        p.op("vector", "tensor_mul", st1[:, 4:5], st1[:, 2:3], st1[:, 2:3])
        p.op("vector", "tensor_sub", st1[:, 5:6], st1[:, 3:4], st1[:, 4:5])
        p.op("vector", "tensor_scalar_add", st1[:, 6:7], st1[:, 5:6], EPS)
        p.op("vector", "reciprocal", st1[:, 7:8], st1[:, 6:7], inc=vp)
        vv = vp.v
        p.wait("scalar", vp, vv)
        p.op("scalar", "activation", r_out, st1[:, 7:8], AF.Sqrt, inc=sp_)

    for i in range(NT):
        ub = u_t[i % 2]
        p.dma("sync", ub[:], u_in[i * 128:(i + 1) * 128, :], qs)
        p.wait("vector", qs, qs.v)
        p.wait("vector", qg, qg.v)
        p.wait("scalar", qs, qs.v)
        ln_stats(ub[:], sd_t[:, 0:1])
        p.wait("vector", sp_, sp_.v)
        p.op("vector", "tensor_scalar", xn_t[:], ub[:], st1[:, 2:3],
             sd_t[:, 0:1], OP.subtract, OP.mult, inc=vp)
        p.wait("vector", qg, qg.v)
        p.op("vector", "tensor_add", base_t[i % 2][:], xn_t[:], ub[:], inc=vp)
        p.wait("sync", vp, vp.v)
        p.wait("gpsimd", vp, vp.v)
        p.dma("sync", xn_tm[i * 128:(i + 1) * 128, :], xn_t[:], qs)
        p.dma("gpsimd", out[i * 128:(i + 1) * 128, :], base_t[i % 2][:], qg)
        ln_stats(xn_t[:], sd_t[:, 1:2])
        p.wait("vector", sp_, sp_.v)
        p.op("vector", "tensor_scalar", un_t[:], xn_t[:], st1[:, 2:3],
             sd_t[:, 1:2], OP.subtract, OP.mult)
        p.op("vector", "tensor_mul", un_t[:], un_t[:], lnwB[:])
        p.op("vector", "tensor_add", unb_t[:], un_t[:], lnbB[:], inc=vp)
        p.wait("gpsimd", vp, vp.v)
        p.dma("gpsimd", un_tm[i * 128:(i + 1) * 128, :], unb_t[:], qg)
        p.wait("tensor", vp, vp.v)
        copy_done = []
        for j in range(16):
            bank = ps[:, (j % 2) * 512:(j % 2) * 512 + 128]
            if j >= 2:
                p.wait("tensor", vp, copy_done[j - 2])
            p.op("tensor", "transpose", bank, xn_t[:, j * 128:(j + 1) * 128],
                 ident_f[:], inc=tp)
            p.wait("vector", tp, tp.v)
            p.op("vector", "tensor_copy", xf_t[:, j * 128:(j + 1) * 128], bank,
                 inc=vp)
            copy_done.append(vp.v)
        p.wait("sync", vp, vp.v)
        p.dma("sync", xn_fm32.rearrange("(c q) t -> q c t", q=128)
              [:, :, i * 128:(i + 1) * 128], xf_t[:].rearrange(
                  "q (c t) -> q c t", c=16), qs)
    p.barrier()
    A.release()
    A.mark()
    unf_st = [A.alloc([128, S], BF16) for _ in range(2)]
    store_done = {}
    for j in range(16):
        t = unf_st[j % 2]
        if j >= 2:
            p.wait("sync", qs, store_done[j - 2])
        p.dmat("sync", t[:], un_tm[:, j * 128:(j + 1) * 128], qs)
        p.wait("sync", qs, qs.v)
        p.dma("sync", un_fm[j * 128:(j + 1) * 128, :], t[:], qs)
        store_done[j] = qs.v
    p.barrier()
    A.release()

    if stop == "ph0":
        blk = stack.enter_context(nc.Block())
        p.finalize(blk)
        stack.close()
        return nc

    A.mark()
    Bn = A.alloc([DS, S], BF16)
    Cn = A.alloc([DS, S], BF16)
    ec_cm = A.alloc([64, NCH, NH], F32)
    ECSb = A.alloc([128, NCH, NH], F32)
    A.mark()
    wbc = A.alloc([128, 16, 320], BF16)
    einv_cm = A.alloc([64, NCH, NH], F32)
    unf = A.alloc([128, 16, 2048], BF16)
    Bf = A.alloc([128, S], F32)
    Cf = A.alloc([128, S], F32)
    dtf = A.alloc([64, S], F32)
    sqB = A.alloc([128, S], F32)
    rowA = A.alloc([1, S], F32)
    rBb = A.alloc([128, S], F32)
    ones_f = A.alloc([128, 1], F32)
    Mb = rBb[0:64, :]
    daf = Bf[0:64, :]
    srow = dtf[0:1, :]
    erow = Cf[0:1, :]
    dacm = Cf[0:64, :]
    cumacm = sqB[0:64, :]

    p.op("gpsimd", "memset", ones_f[:], 1.0, inc=gp)
    for dc in range(16):
        p.dma("gpsimd", wbc[:, dc, :],
              in_projT[dc * 128:(dc + 1) * 128, 8192:8512], qg)
        p.dma("sync", unf[:, dc, :], un_fm[dc * 128:(dc + 1) * 128, :], qs)
    p.wait("tensor", qg, qg.v)
    p.wait("tensor", qs, qs.v)
    p.wait("tensor", gp, gp.v)

    def proj_part(e0, e1, dst, banksel):
        m = e1 - e0
        evs = []
        for tt in range(4):
            bank = ps[0:m, (banksel * 4 + tt) * 512:(banksel * 4 + tt) * 512 + 512]
            for dc in range(16):
                p.op("tensor", "matmul", bank, wbc[:, dc, e0:e1],
                     unf[:, dc, tt * 512:(tt + 1) * 512],
                     start=(dc == 0), stop=(dc == 15),
                     inc=(tp if dc == 15 else None))
            p.wait("vector", tp, tp.v)
            p.op("vector", "tensor_copy", dst[:, tt * 512:(tt + 1) * 512], bank,
                 inc=vp)
            evs.append(vp.v)
        return evs

    proj_part(0, 128, Bf, 0)
    proj_part(128, 256, Cf, 1)
    p.wait("tensor", vp, vp.v)
    proj_part(256, 320, dtf, 0)

    def rms_norm(src, w_ap, dst_bf, rtrow):
        p.wait("scalar", vp, vp.v)
        p.op("scalar", "activation", sqB[:], src[:], AF.Square, inc=sp_)
        p.wait("tensor", sp_, sp_.v)
        p.wait("tensor", vp, vp.v)
        for tt in range(4):
            bank = ps[0:1, (4 + tt) * 512:(4 + tt) * 512 + 512]
            p.op("tensor", "matmul", bank, ones_f[:],
                 sqB[:, tt * 512:(tt + 1) * 512], start=True, stop=True,
                 inc=tp)
            p.wait("vector", tp, tp.v)
            p.op("vector", "tensor_copy", rowA[:, tt * 512:(tt + 1) * 512],
                 bank, inc=vp)
        p.op("vector", "tensor_scalar", rowA[:], rowA[:], 1.0 / DS, EPS,
             OP.mult, OP.add, inc=vp)
        p.wait("scalar", vp, vp.v)
        p.op("scalar", "activation", rowA[:], rowA[:], AF.Sqrt, inc=sp_)
        p.wait("vector", sp_, sp_.v)
        p.op("vector", "reciprocal", rowA[:], rowA[:], inc=vp)
        p.wait("sync", vp, vp.v)
        p.dma("sync", rt[rtrow:rtrow + 1, :], rowA[:], qs)
        p.wait("sync", qs, qs.v)
        p.dma("sync", rBb[:], rt[rtrow:rtrow + 1, :].partition_broadcast(128),
              qs)
        p.wait("vector", qs, qs.v)
        p.op("vector", "tensor_mul", sqB[:], src[:], rBb[:])
        p.op("vector", "tensor_scalar", dst_bf[:], sqB[:], w_ap, None,
             OP.mult, inc=vp)

    rms_norm(Bf, bnw[:], Bn, 0)
    rms_norm(Cf, cnw[:], Cn, 1)

    p.op("scalar", "activation", negA[:], alog[:], AF.Exp, inc=sp_)
    p.wait("vector", sp_, sp_.v)
    p.op("vector", "tensor_scalar_mul", negA[:], negA[:], -1.0, inc=vp)
    p.wait("scalar", vp, vp.v)
    p.op("scalar", "activation", daf[:], dtf[:], AF.Exp, bias=dbias[:])
    p.op("scalar", "activation", daf[:], daf[:], AF.Ln, bias=1.0, inc=sp_)
    p.wait("vector", sp_, sp_.v)
    p.op("vector", "tensor_scalar", daf[:], daf[:], negA[:], None, OP.mult,
         inc=vp)
    p.wait("tensor", vp, vp.v)
    cdone = []
    for c in range(NCH):
        bank = ps[0:64, (c % 2) * 512:(c % 2) * 512 + 64]
        if c >= 2:
            p.wait("tensor", vp, cdone[c - 2])
        p.op("tensor", "transpose", bank, daf[:, c * 64:(c + 1) * 64],
             ident_f[0:64, 0:64], inc=tp)
        p.wait("vector", tp, tp.v)
        p.op("vector", "tensor_copy", dacm[:, c * 64:(c + 1) * 64], bank,
             inc=vp)
        cdone.append(vp.v)
    p.wait("tensor", vp, vp.v)
    for tt in range(4):
        bank = ps[0:64, (2 + tt) * 512:(2 + tt) * 512 + 512]
        p.op("tensor", "matmul", bank, tri_f[:],
             dacm[:, tt * 512:(tt + 1) * 512], start=True, stop=True, inc=tp)
        p.wait("vector", tp, tp.v)
        p.op("vector", "tensor_copy", cumacm[:, tt * 512:(tt + 1) * 512],
             bank, inc=vp)
    p.wait("sync", vp, vp.v)
    p.dma("sync", rt[3:4, :], cumacm[0:1, :], qs)
    p.dma("sync", rt[4:5, :], cumacm[63:64, :], qs)
    p.wait("sync", qs, qs.v)
    p.dma("sync", rowA[:], rt[3:4, :], qs)
    p.dma("sync", srow[:], rt[4:5, :], qs)
    p.wait("vector", qs, qs.v)
    p.op("vector", "tensor_add", rowA[:], rowA[:], srow[:])
    p.op("vector", "tensor_scalar_mul", rowA[:], rowA[:], 0.5)
    p.op("vector", "tensor_sub", erow[:], srow[:], rowA[:])
    e3 = erow[:].rearrange("p (c h) -> p c h", c=NCH)
    m3 = rowA[:].rearrange("p (c h) -> p c h", c=NCH)
    p.op("vector", "tensor_add", e3[:, 0:NCH - 1, :], e3[:, 0:NCH - 1, :],
         m3[:, 1:NCH, :], inc=vp)
    p.wait("scalar", vp, vp.v)
    p.op("scalar", "activation", erow[:], erow[:], AF.Exp, inc=sp_)
    p.wait("sync", sp_, sp_.v)
    p.dma("sync", rt[2:3, :], erow[:], qs)
    p.dma("sync", rt[5:6, :], rowA[:], qs)
    p.wait("sync", qs, qs.v)
    p.dma("sync", ECSb.rearrange("p a b -> p (a b)"),
          rt[2:3, :].partition_broadcast(128), qs)
    p.dma("sync", Mb[:], rt[5:6, :].partition_broadcast(64), qs)
    p.wait("vector", qs, qs.v)
    p.op("vector", "tensor_sub", daf[:], Mb[:], cumacm[:])
    p.op("vector", "tensor_scalar_min", daf[:], daf[:], 85.0, inc=vp)
    p.wait("scalar", vp, vp.v)
    p.op("scalar", "activation", einv_cm.rearrange("p a b -> p (a b)"),
         daf[:], AF.Exp, inc=sp_)
    p.wait("vector", sp_, sp_.v)
    p.op("vector", "tensor_sub", daf[:], cumacm[:], Mb[:])
    p.op("vector", "tensor_scalar_min", daf[:], daf[:], 85.0, inc=vp)
    p.wait("scalar", vp, vp.v)
    p.op("scalar", "activation", ec_cm.rearrange("p a b -> p (a b)"),
         daf[:], AF.Exp, inc=sp_)
    p.wait("vector", sp_, sp_.v)
    p.wait("sync", sp_, sp_.v)

    p.wait("sync", sp_, sp_.v)
    p.dma("sync", einv_dram, einv_cm.rearrange("p a b -> p (a b)"), qs)

    if "bn_tap" in taps:
        bn_tap = dram("bn_tap", [DS, S], BF16)
        cn_tap = dram("cn_tap", [DS, S], BF16)
        cuma_tap = dram("cuma_tap", [64, S], F32)
        p.wait("sync", vp, vp.v)
        p.dma("sync", bn_tap, Bn[:], qs)
        p.dma("sync", cn_tap, Cn[:], qs)
        p.dma("sync", cuma_tap, cumacm[:], qs)

    p.barrier()
    A.release()

    einv_tm = A.alloc([128, 16, NH], F32)
    wxz0 = A.alloc([128, 16, 1024], BF16)
    mmsk_all = A.alloc([64, NCH, 64], BF16)
    btb_all = A.alloc([64, NCH, DS], BF16)
    _ev = einv_dram.rearrange("l (pr two h) -> l pr two h", two=2, h=NH)
    p.dma("sync", einv_tm[0:64, :, :], _ev[:, :, 0, :], qs)
    p.dma("sync", einv_tm[64:128, :, :], _ev[:, :, 1, :], qs)
    for dc in range(16):
        p.dma("gpsimd", wxz0[:, dc, 0:512],
              in_projT[dc * 128:(dc + 1) * 128, 4096:4608], qg)
        p.dma("gpsimd", wxz0[:, dc, 512:1024],
              in_projT[dc * 128:(dc + 1) * 128, 0:512], qg)
    wxz0_rdy = qg.v

    if stop == "ph1":
        blk = stack.enter_context(nc.Block())
        p.finalize(blk)
        stack.close()
        return nc

    A.mark()
    p.wait("tensor", qs, qs.v)
    sc_ev = {}
    for c in range(NCH):
        b_sc = ps[0:64, (c % 2) * 512:(c % 2) * 512 + 64]
        b_bt = ps_bf[0:64, (2 + c % 2) * 1024:(2 + c % 2) * 1024 + 128]
        if c >= 2:
            p.wait("tensor", vp, sc_ev[c - 2])
        p.op("tensor", "matmul", b_sc, Bn[:, c * 64:(c + 1) * 64],
             Cn[:, c * 64:(c + 1) * 64], start=True, stop=True)
        p.op("tensor", "transpose", b_bt, Bn[:, c * 64:(c + 1) * 64],
             ident_bf[:], inc=tp)
        p.wait("vector", tp, tp.v)
        p.op("vector", "tensor_mul", mmsk_all[:, c, :], b_sc, tri_f[:])
        p.op("vector", "tensor_copy", btb_all[:, c, :], b_bt, inc=vp)
        sc_ev[c] = vp.v

    wxz1 = A.alloc([128, 16, 1024], BF16)
    wxz = [wxz0, wxz1]
    unp = [A.alloc([128, 16, 128], BF16) for _ in range(2)]
    pairb = [A.alloc([128, 3, 512], BF16) for _ in range(2)]
    shb = [A.alloc([64, 3, 512], BF16) for _ in range(2)]
    stq = [A.alloc([128, 512], BF16) for _ in range(2)]
    t1 = A.alloc([64, 512], F32)
    ygb = [A.alloc([64, 512], BF16) for _ in range(2)]

    qsc = p.sems["q_scalar"]
    wxz_rdy = {0: wxz0_rdy}
    shd_par = {0: 0, 1: 0}
    wxz_used = {}
    xz_ev = {}
    yg_dma = {}
    sh_done = {}
    st_up = {}
    yev = {}
    pe_ssd = {}
    for e8 in range(8):
        H0 = 8 * e8
        EX = 4096 + 512 * e8
        EZ = 512 * e8
        par = e8 % 2
        p.wait("vector", tp, tp.v)
        p.op("vector", "memset", stq[0][:], 0.0)
        p.op("vector", "memset", stq[1][:], 0.0, inc=vp)
        un_rdy = {}
        for pr in range(17):
            if pr < 16 and e8 < 7:
                dc = pr
                if (e8 - 1) in wxz_used:
                    p.wait("gpsimd", tp, wxz_used[e8 - 1])
                p.dma("gpsimd", wxz[1 - par][:, dc, 0:512],
                      in_projT[dc * 128:(dc + 1) * 128,
                               EX + 512:EX + 1024], qg)
                p.dma("gpsimd", wxz[1 - par][:, dc, 512:1024],
                      in_projT[dc * 128:(dc + 1) * 128, EZ + 512:EZ + 1024],
                      qg)
                wxz_rdy[e8 + 1] = qg.v
            if pr < 16:
                if pr >= 2:
                    p.wait("sync", tp, xz_ev.get(("zmm", pr - 2), 0))
                un_rdy[pr] = p.dma(
                    "sync", unp[pr % 2][:],
                    un_fm.rearrange("(c pp) t -> pp c t", pp=128)
                    [:, :, pr * 128:(pr + 1) * 128], qs)
            if pr > 0:
                cp = pr - 1
                for ci in range(2):
                    c = 2 * cp + ci
                    if ci == 0:
                        xt_s = pairb[cp % 2][0:64, 0, :]
                        xd_s = pairb[cp % 2][0:64, 1, :]
                        sz_s = pairb[cp % 2][0:64, 2, :]
                    else:
                        xt_s = shb[cp % 2][:, 0, :]
                        xd_s = shb[cp % 2][:, 1, :]
                        sz_s = shb[cp % 2][:, 2, :]
                    stcur = stq[c % 2]
                    stnew = stq[1 - c % 2]
                    yb = ps[0:64, 1024 + ci * 512:1536 + ci * 512]
                    pb = ps[:, 2048 + ci * 512:2560 + ci * 512]
                    if ci == 1:
                        p.wait("tensor", qsc, shd_par[cp % 2])
                    p.wait("tensor", vp, max(
                        xz_ev.get(("xev", cp), 0), yev.get(c - 2, 0),
                        st_up.get(c - 1, 0)))
                    p.wait("tensor", sp_, xz_ev.get(("silu", cp), 0))
                    p.op("tensor", "matmul", yb, mmsk_all[:, c, :], xt_s,
                         start=True, stop=False)
                    p.op("tensor", "matmul", yb, Cn[:, c * 64:(c + 1) * 64],
                         stcur[:], start=False, stop=True)
                    p.op("tensor", "matmul", pb, btb_all[:, c, :], xt_s,
                         start=True, stop=True, inc=tp)
                    pe_ssd[c] = tp.v
                    p.wait("vector", tp, pe_ssd[c])
                    p.op("vector", "tensor_tensor", stnew[:], pb, stcur[:],
                         OP.add)
                    p.op("vector", "tensor_tensor",
                         stnew[:].rearrange("p (h j) -> p h j", h=8),
                         stnew[:].rearrange("p (h j) -> p h j", h=8),
                         ECSb[:, c, H0:H0 + 8].unsqueeze(2)
                         .broadcast_to([128, 8, 64]), OP.mult, inc=vp)
                    st_up[c] = vp.v
                    p.wait("vector", qs, yg_dma.get(c - 2, 0))
                    p.op("vector", "tensor_tensor",
                         t1[:].rearrange("p (h j) -> p h j", h=8),
                         yb.rearrange("p (h j) -> p h j", h=8),
                         ec_cm[:, c, H0:H0 + 8].unsqueeze(2)
                         .broadcast_to([64, 8, 64]), OP.mult)
                    p.op("vector", "tensor_add", t1[:], t1[:], xd_s)
                    p.op("vector", "tensor_mul", ygb[c % 2][:], t1[:], sz_s,
                         inc=vp)
                    yev[c] = vp.v
                    p.wait("sync", vp, vp.v)
                    yg_dma[c] = p.dma(
                        "sync", yg_tm[c * 64:(c + 1) * 64,
                                      512 * e8:512 * (e8 + 1)],
                        ygb[c % 2][:], qs)
            if pr < 16:
                p.wait("tensor", qs, un_rdy[pr])
                if pr == 0:
                    p.wait("tensor", qg, wxz_rdy[e8])
                p.wait("tensor", vp, xz_ev.get(("xev", pr - 1), 0))
                p.wait("tensor", sp_, xz_ev.get(("silu", pr - 1), 0))
                for dc in range(16):
                    p.op("tensor", "matmul", ps[:, 0:512],
                         unp[pr % 2][:, dc, :], wxz[par][:, dc, 0:512],
                         start=(dc == 0), stop=(dc == 15),
                         inc=(tp if dc == 15 else None))
                xz_ev[("xmm", pr)] = tp.v
                for dc in range(16):
                    p.op("tensor", "matmul", ps[:, 512:1024],
                         unp[pr % 2][:, dc, :], wxz[par][:, dc, 512:1024],
                         start=(dc == 0), stop=(dc == 15),
                         inc=(tp if dc == 15 else None))
                xz_ev[("zmm", pr)] = tp.v
                if pr == 15:
                    wxz_used[e8] = tp.v
                p.wait("vector", tp, xz_ev[("xmm", pr)])
                p.wait("vector", qsc, shd_par[pr % 2])
                p.op("vector", "tensor_tensor",
                     pairb[pr % 2][:, 0, :].rearrange("p (h j) -> p h j", h=8),
                     ps[:, 0:512].rearrange("p (h j) -> p h j", h=8),
                     einv_tm[:, pr, H0:H0 + 8].unsqueeze(2)
                     .broadcast_to([128, 8, 64]), OP.mult)
                p.op("vector", "tensor_tensor",
                     pairb[pr % 2][:, 1, :].rearrange("p (h j) -> p h j", h=8),
                     ps[:, 0:512].rearrange("p (h j) -> p h j", h=8),
                     Dbc[:, H0:H0 + 8].unsqueeze(2)
                     .broadcast_to([128, 8, 64]), OP.mult, inc=vp)
                xz_ev[("xev", pr)] = vp.v
                p.wait("scalar", tp, xz_ev[("zmm", pr)])
                p.wait("scalar", qsc, shd_par[pr % 2])
                p.op("scalar", "activation", pairb[pr % 2][:, 2, :],
                     ps[:, 512:1024], AF.Silu, inc=sp_)
                xz_ev[("silu", pr)] = sp_.v
                p.wait("scalar", sp_, xz_ev[("silu", pr)])
                p.wait("scalar", vp, xz_ev[("xev", pr)])
                p.wait("scalar", tp, pe_ssd.get(2 * pr - 1, 0))
                p.wait("scalar", vp, yev.get(2 * pr - 1, 0))
                shd_par[pr % 2] = p.dma("scalar", shb[pr % 2][:],
                                        pairb[pr % 2][64:128, :, :], qsc)
    p.barrier()
    A.release()
    A.release()

    if stop == "ph2":
        blk = stack.enter_context(nc.Block())
        p.finalize(blk)
        stack.close()
        return nc

    A.mark()
    ygf3 = A.alloc([128, 16, 2048], BF16)
    wob3 = A.alloc([128, 16, 2048], BF16)
    stage = [A.alloc([128, 2048], F32) for _ in range(2)]
    ev_done = {}
    mm3_done = 0
    out_w = 0
    for hf in range(2):
        ygf_rdy = wob_rdy = 0
        for ec in range(16):
            gc = hf * 16 + ec
            if hf > 0:
                p.wait("sync", tp, mm3_done)
                p.wait("gpsimd", tp, mm3_done)
            ygf_rdy = p.dmat("sync", ygf3[:, ec, :],
                             yg_tm[:, gc * 128:(gc + 1) * 128], qs)
            wob_rdy = p.dma("gpsimd", wob3[:, ec, :],
                            woutT[gc * 128:(gc + 1) * 128, :], qg)
        p.wait("tensor", qs, ygf_rdy)
        p.wait("tensor", qg, wob_rdy)
        if hf > 0:
            p.wait("gpsimd", qg, out_w)
        for tc in range(16):
            for dt_ in range(4):
                bank = ps[:, (dt_ % 2) * 512:(dt_ % 2) * 512 + 512]
                prev = (tc, dt_ - 2) if dt_ >= 2 else (tc - 1, dt_ + 2)
                if prev in ev_done:
                    p.wait("tensor", vp, ev_done[prev])
                for ec in range(16):
                    p.op("tensor", "matmul", bank,
                         ygf3[:, ec, tc * 128:(tc + 1) * 128],
                         wob3[:, ec, dt_ * 512:(dt_ + 1) * 512],
                         start=(ec == 0), stop=(ec == 15),
                         inc=(tp if ec == 15 else None))
                p.wait("vector", tp, tp.v)
                if dt_ == 0 and (tc >= 2 or hf > 0):
                    p.wait("vector", qg, ev_done.get(("dma", tc - 2, hf), 0)
                           if tc >= 2 else out_w)
                sg = stage[tc % 2][:, dt_ * 512:(dt_ + 1) * 512]
                p.op("vector", "tensor_copy", sg, bank, inc=vp)
                ev_done[(tc, dt_)] = vp.v
            p.wait("gpsimd", vp, vp.v)
            ev_done[("dma", tc, hf)] = out_w = p.dma(
                "gpsimd", out[tc * 128:(tc + 1) * 128, :], stage[tc % 2][:],
                qg, accum_op=OP.add)
        mm3_done = tp.v
    p.barrier()
    A.release()

    if stop == "ph3":
        blk = stack.enter_context(nc.Block())
        p.finalize(blk)
        stack.close()
        return nc

    A.mark()
    xcol = [A.alloc([128, 16, 128], F32) for _ in range(2)]
    gate_sb = A.alloc([128, 16, 8], F32)
    sgt_sb = A.alloc([128, 16], F32)
    lg = A.alloc([128, 16, 8], F32)
    ro = A.alloc([128, 16, 8], F32)
    rank = A.alloc([128, 16, 8], F32)
    tmp4 = A.alloc([128, 16, 8], F32)
    msk4 = A.alloc([128, 16, 8], F32)
    mx4 = A.alloc([128, 16, 1], F32)
    io8 = A.alloc([128, 8], F32)
    io8i = A.alloc([128, 8], mybir.dt.int32)
    tb = A.alloc([128, 7, 8], F32)
    sgrow = A.alloc([1, S], F32)
    combf = A.alloc([8, S], F32)

    p.dma("sync", gate_sb[:],
          gateT.rearrange("(c p) e -> p c e", p=128), qs)
    p.dma("sync", sgt_sb[:], sgT, qs)
    p.op("gpsimd", "iota", io8i[:], pattern=[[1, 8]], base=0,
         channel_multiplier=0, inc=gp)
    p.wait("vector", gp, gp.v)
    p.op("vector", "tensor_copy", io8[:], io8i[:])
    for s_ in range(1, 8):
        p.op("vector", "tensor_single_scalar", tb[:, s_ - 1, :], io8[:],
             float(s_), OP.is_ge)
    ld_prev = 0
    for tc in range(16):
        xc4 = xcol[tc % 2]
        if tc >= 2:
            p.wait("sync", tp, ld_prev)
        p.dma("sync", xc4[:],
              xn_fm32.rearrange("(c p) t -> p c t", p=128)
              [:, :, tc * 128:(tc + 1) * 128], qs)
        p.wait("tensor", qs, qs.v)
        bl = ps[:, 0:8]
        bs = ps[0:1, 512:640]
        for dc in range(16):
            p.op("tensor", "matmul", bl, xc4[:, dc, :], gate_sb[:, dc, :],
                 start=(dc == 0), stop=(dc == 15),
                 inc=(tp if dc == 15 else None))
        for dc in range(16):
            p.op("tensor", "matmul", bs, sgt_sb[:, dc:dc + 1], xc4[:, dc, :],
                 start=(dc == 0), stop=(dc == 15),
                 inc=(tp if dc == 15 else None))
        ld_prev = tp.v
        p.wait("vector", tp, tp.v)
        p.op("vector", "tensor_copy", lg[:, tc, :], bl)
        p.op("vector", "tensor_copy", sgrow[:, tc * 128:(tc + 1) * 128], bs,
             inc=vp)
        p.wait("tensor", vp, vp.v)
    p.op("vector", "memset", rank[:], 0.0)
    for s_ in range(1, 8):
        p.op("vector", "tensor_copy", ro[:, :, s_:8], lg[:, :, 0:8 - s_])
        p.op("vector", "tensor_copy", ro[:, :, 0:s_], lg[:, :, 8 - s_:8])
        p.op("vector", "tensor_tensor", tmp4[:], ro[:], lg[:], OP.is_gt)
        p.op("vector", "tensor_add", rank[:], rank[:], tmp4[:])
        p.op("vector", "tensor_tensor", tmp4[:], ro[:], lg[:], OP.is_equal)
        p.op("vector", "tensor_mul", tmp4[:], tmp4[:],
             tb[:, s_ - 1:s_, :].broadcast_to([128, 16, 8]))
        p.op("vector", "tensor_add", rank[:], rank[:], tmp4[:])
    p.op("vector", "tensor_single_scalar", msk4[:], rank[:], 3.0, OP.is_le)
    p.op("vector", "reduce_max", mx4[:], lg[:], axis=AX.X)
    p.op("vector", "tensor_sub", tmp4[:], lg[:],
         mx4[:].broadcast_to([128, 16, 8]), inc=vp)
    p.wait("scalar", vp, vp.v)
    p.op("scalar", "activation", tmp4[:], tmp4[:], AF.Exp, inc=sp_)
    p.wait("vector", sp_, sp_.v)
    p.op("vector", "tensor_mul", tmp4[:], tmp4[:], msk4[:])
    p.op("vector", "reduce_sum", mx4[:], tmp4[:], axis=AX.X)
    p.op("vector", "reciprocal", mx4[:], mx4[:])
    p.op("vector", "tensor_mul", tmp4[:], tmp4[:],
         mx4[:].broadcast_to([128, 16, 8]), inc=vp)
    p.wait("tensor", vp, vp.v)
    cdone = []
    for tc in range(16):
        bank = ps[0:8, (tc % 2) * 512:(tc % 2) * 512 + 128]
        if tc >= 2:
            p.wait("tensor", vp, cdone[tc - 2])
        p.op("tensor", "transpose", bank, tmp4[:, tc, :], ident_f[:], inc=tp)
        p.wait("vector", tp, tp.v)
        p.op("vector", "tensor_copy", combf[:, tc * 128:(tc + 1) * 128], bank,
             inc=vp)
        cdone.append(vp.v)
    p.wait("sync", vp, vp.v)
    p.dma("sync", comb_fm[0:8, :], combf[:], qs)
    p.op("scalar", "activation", sgrow[:], sgrow[:], AF.Sigmoid, inc=sp_)
    p.wait("sync", sp_, sp_.v)
    for g in range(4):
        p.dma("sync", comb_fm[8 + g:9 + g, :], sgrow[:], qs)
    p.barrier()
    A.release()

    if stop == "ph4":
        blk = stack.enter_context(nc.Block())
        p.finalize(blk)
        stack.close()
        return nc

    A.mark()
    xnb = A.alloc([128, 16, 2048], BF16)
    actk = A.alloc([128, 11, 2048], BF16)
    wdk = A.alloc([128, 11, 2048], BF16)
    combB = A.alloc([128, 2048], BF16)
    t5sg = [A.alloc([128, 512], F32) for _ in range(2)]
    for dc in range(16):
        p.dma("gpsimd", xnb[:, dc, :], xn_fm32[dc * 128:(dc + 1) * 128, :], qg)
    xnb_rdy = qg.v
    wgc = [A.alloc([128, 16, 128], BF16) for _ in range(2)]
    wuc = [A.alloc([128, 16, 128], BF16) for _ in range(2)]
    dsA = [A.alloc([128, 1024], F32) for _ in range(2)]
    dsB = [A.alloc([128, 1024], F32) for _ in range(2)]

    def exp_srcs(k, ic):
        if k < 8:
            return (wgT[k][:, ic * 128:(ic + 1) * 128],
                    wuT[k][:, ic * 128:(ic + 1) * 128],
                    wdT[k][ic * 128:(ic + 1) * 128, :])
        g = k - 8
        o = g * I_EXP + ic * 128
        return (swgT[:, o:o + 128], swuT[:, o:o + 128], swdT[o:o + 128, :])

    wg_s, wu_s, _ = exp_srcs(0, 0)
    p.dma("gpsimd", wgc[0][:], wg_s.rearrange("(c p) i -> p c i", p=128), qg)
    p.dma("gpsimd", wuc[0][:], wu_s.rearrange("(c p) i -> p c i", p=128), qg)
    w_rdy = {(0, 0): qg.v}

    prev_accum = 0
    up_ev_prev = 0
    buf_last_use = {0: 0, 1: 0}
    slice_ev = {}
    up_bank_ev = []
    for k in range(NEXP):
        p.wait("gpsimd", vp, up_ev_prev)
        comb_rdy = p.dma("gpsimd", combB[:],
                         comb_fm[k:k + 1, :].partition_broadcast(128), qg)
        wd_rdy = []
        for ic in range(11):
            _, _, wd_s = exp_srcs(k, ic)
            wd_rdy.append(p.dma("gpsimd", wdk[:, ic, :], wd_s, qg))
        for ic in range(11):
            nk, nic = (k, ic + 1) if ic < 10 else (k + 1, 0)
            if nk < NEXP:
                par = (ic + 1) % 2
                p.wait("gpsimd", tp, buf_last_use[par])
                wg_s, wu_s, _ = exp_srcs(nk, nic)
                p.dma("gpsimd", wgc[par][:],
                      wg_s.rearrange("(c p) i -> p c i", p=128), qg)
                p.dma("gpsimd", wuc[par][:],
                      wu_s.rearrange("(c p) i -> p c i", p=128), qg)
                w_rdy[(nk, nic)] = qg.v
            p.wait("tensor", qg, w_rdy[(k, ic)])
            if k == 0 and ic == 0:
                p.wait("tensor", qg, xnb_rdy)
            for tt in range(4):
                bg = ps[:, (tt % 2) * 1024:(tt % 2) * 1024 + 512]
                bu = ps[:, (tt % 2) * 1024 + 512:(tt % 2) * 1024 + 1024]
                if len(up_bank_ev) >= 2:
                    p.wait("tensor", vp, up_bank_ev[-2])
                for dc in range(16):
                    p.op("tensor", "matmul", bg, wgc[ic % 2][:, dc, :],
                         xnb[:, dc, tt * 512:(tt + 1) * 512],
                         start=(dc == 0), stop=(dc == 15))
                for dc in range(16):
                    p.op("tensor", "matmul", bu, wuc[ic % 2][:, dc, :],
                         xnb[:, dc, tt * 512:(tt + 1) * 512],
                         start=(dc == 0), stop=(dc == 15),
                         inc=(tp if dc == 15 else None))
                buf_last_use[ic % 2] = tp.v
                p.wait("scalar", tp, tp.v)
                p.wait("scalar", vp, slice_ev.get(tt % 2, 0))
                sgb = t5sg[tt % 2]
                p.op("scalar", "activation", sgb[:], bg, AF.Silu, inc=sp_)
                p.wait("vector", sp_, sp_.v)
                if ic == 0 and tt == 0:
                    p.wait("vector", qg, comb_rdy)
                sl = slice(tt * 512, (tt + 1) * 512)
                p.op("vector", "tensor_mul", sgb[:], sgb[:], bu)
                p.op("vector", "tensor_mul", actk[:, ic, sl], sgb[:],
                     combB[:, sl], inc=vp)
                slice_ev[tt % 2] = vp.v
                up_bank_ev.append(vp.v)
        up_ev_prev = vp.v
        p.wait("tensor", qg, wd_rdy[-1])
        p.wait("tensor", vp, up_ev_prev)
        if k > 0:
            p.wait("gpsimd", qg, prev_accum)
        ds_dma = {}
        dn_ev = {}
        for tc in range(16):
            for dt_ in range(4):
                bank = ps[:, 2048 + (dt_ % 2) * 512:2048 + (dt_ % 2) * 512 + 512]
                prev = (tc, dt_ - 2) if dt_ >= 2 else (tc - 1, dt_ + 2)
                if prev in dn_ev:
                    p.wait("tensor", vp, dn_ev[prev])
                for ic in range(11):
                    p.op("tensor", "matmul", bank,
                         actk[:, ic, tc * 128:(tc + 1) * 128],
                         wdk[:, ic, dt_ * 512:(dt_ + 1) * 512],
                         start=(ic == 0), stop=(ic == 10),
                         inc=(tp if ic == 10 else None))
                p.wait("vector", tp, tp.v)
                if dt_ == 0 and tc >= 2:
                    p.wait("vector", qg, ds_dma[tc - 2])
                half = dsA[tc % 2] if dt_ < 2 else dsB[tc % 2]
                p.op("vector", "tensor_copy",
                     half[:, (dt_ % 2) * 512:(dt_ % 2) * 512 + 512], bank,
                     inc=vp)
                dn_ev[(tc, dt_)] = vp.v
            p.wait("gpsimd", vp, vp.v)
            p.dma("gpsimd", out[tc * 128:(tc + 1) * 128, 0:1024],
                  dsA[tc % 2][:], qg, accum_op=OP.add)
            ds_dma[tc] = p.dma("gpsimd",
                               out[tc * 128:(tc + 1) * 128, 1024:2048],
                               dsB[tc % 2][:], qg, accum_op=OP.add)
        prev_accum = qg.v
    p.barrier()
    A.release()

    blk = stack.enter_context(nc.Block())
    p.finalize(blk)
    stack.close()
    return nc



def kernel(**inputs):
    w = inputs
    c = np.ascontiguousarray
    in_projT = c(w["L0_attn_in_proj_w"].T)
    woutT = c(w["L0_attn_out_proj_w"].T)
    gateT = c(w["L0_ffn_gate_w"].T)
    sgT = c(w["L0_ffn_shared_gate_w"].reshape(16, 128).T)
    wgT = c(w["L0_ffn_wg"].transpose(0, 2, 1))
    wuT = c(w["L0_ffn_wu"].transpose(0, 2, 1))
    wdT = c(w["L0_ffn_wd"].transpose(0, 2, 1))
    swgT = c(w["L0_ffn_swg"].T)
    swuT = c(w["L0_ffn_swu"].T)
    swdT = c(w["L0_ffn_swd"].T)
    base = {
        "in_projT": in_projT, "woutT": woutT, "gateT": gateT, "sgT": sgT,
        "wgT": wgT, "wuT": wuT, "wdT": wdT, "swgT": swgT, "swuT": swuT,
        "swdT": swdT,
        "Dh": c(w["L0_attn_D"].reshape(1, NH)),
        "dt_bias": c(w["L0_attn_dt_bias"].reshape(NH, 1)),
        "A_log": c(w["L0_attn_A_log"].reshape(NH, 1)),
        "bnorm_w": c(w["L0_attn_bnorm_w"].reshape(DS, 1)),
        "cnorm_w": c(w["L0_attn_cnorm_w"].reshape(DS, 1)),
        "ln_w": c(w["L0_attn_ln_w"].reshape(1, D)),
        "ln_b": c(w["L0_attn_ln_b"].reshape(1, D)),
    }
    u = w["L0_attn_u"]
    in_maps = []
    for b in range(8):
        m = dict(base)
        m["u"] = c(u[b])
        in_maps.append(m)
    nc = build_program(taps=DEV_TAPS, stop=DEV_STOP)
    res = run_bass_kernel_spmd(nc, in_maps, core_ids=list(range(8)))
    if DEV_TAPS:
        kernel.dev_results = res.results
    return np.stack([r["out"] for r in res.results], axis=0)
